# version 19
# baseline (speedup 1.0000x reference)
import numpy as np
import concourse.bass as bass
import concourse.mybir as mybir

F32 = mybir.dt.float32
BF16 = mybir.dt.bfloat16
AF = mybir.ActivationFunctionType
ALU = mybir.AluOpType

NCORES = 8


class Phase:
    def __init__(self, nc):
        self.nc = nc
        self.eng = {
            "sync": nc.sync,
            "tensor": nc.tensor,
            "vector": nc.vector,
            "scalar": nc.scalar,
            "gpsimd": nc.gpsimd,
        }
        self._ctxs = []
        self.bar_sem = self._sem("bar")
        self.nbar = 0

    def _sem(self, name):
        ctx = self.nc.semaphore(name)
        h = ctx.__enter__()
        self._ctxs.append(ctx)
        return h

    def barrier(self):
        self.nbar += 1
        target = 5 * self.nbar
        for e in self.eng.values():
            e.nop().then_inc(self.bar_sem, 1)
        for e in self.eng.values():
            e.wait_ge(self.bar_sem, target)

    def close(self):
        for ctx in reversed(self._ctxs):
            ctx.__exit__(None, None, None)


class Chain:

    def __init__(self, ph, name):
        self.sem = ph._sem(name)
        self.n = 0

    def inc(self, inst, k=1):
        inst.then_inc(self.sem, k)
        self.n += k
        return self.n

    def wait(self, eng, n=None):
        eng.wait_ge(self.sem, self.n if n is None else n)


class DQ:

    def __init__(self, ph, eng_name, name=None):
        self.e = ph.eng[eng_name]
        self.sem = ph._sem(name or ("dq_" + eng_name))
        self.n = 0

    def start(self, out, in_, serialize=True, **kw):
        if serialize and self.n:
            self.e.wait_ge(self.sem, self.n)
        self.e.dma_start(out=out, in_=in_, **kw).then_inc(self.sem, 16)
        self.n += 16
        return self.n

    def wait_all(self, eng):
        eng.wait_ge(self.sem, self.n)


def ceil_div(a, b):
    return (a + b - 1) // b


def ln_stats(ph, ch, slabs, tcols, ps_a, ps_b, sq_sb, ones_sb,
             mu_row, rstd_row, scr_row, eps_ap=None, nfeat=None,
             cast_bf=None, ones_bf=None):
    nkt = len(slabs)
    nfeat = nfeat if nfeat is not None else nkt * 128
    t, v, s = ph.eng["tensor"], ph.eng["vector"], ph.eng["scalar"]
    if cast_bf is not None:
        ch.wait(v)
        ev_c = {}; mm_c = {}
        for kt in range(nkt):
            sl = kt % 2
            cslot = cast_bf[:, sl * tcols:(sl + 1) * tcols]
            if kt >= 2:
                v.wait_ge(ch.sem, mm_c[kt - 2])
            ch.inc(v.tensor_copy(cslot, slabs[kt]))
            ev_c[kt] = ch.n
            t.wait_ge(ch.sem, ev_c[kt])
            ch.inc(t.matmul(ps_a[0:1, 0:tcols], ones_bf[:, 0:1], cslot,
                            start=(kt == 0), stop=(kt == nkt - 1)))
            mm_c[kt] = ch.n
        ev_q = {}; mm_q = {}
        for kt in range(nkt):
            sl = kt % 2
            qslot = cast_bf[:, (2 + sl) * tcols:(3 + sl) * tcols]
            if kt >= 2:
                s.wait_ge(ch.sem, mm_q[kt - 2])
            ch.inc(s.activation(out=qslot, in_=slabs[kt], func=AF.Square))
            ev_q[kt] = ch.n
            t.wait_ge(ch.sem, ev_q[kt])
            ch.inc(t.matmul(ps_b[0:1, 0:tcols], ones_bf[:, 0:1], qslot,
                            start=(kt == 0), stop=(kt == nkt - 1)))
            mm_q[kt] = ch.n
    else:
        for kt in range(nkt):
            i = t.matmul(ps_a[0:1, 0:tcols], ones_sb[:, 0:1], slabs[kt],
                         start=(kt == 0), stop=(kt == nkt - 1))
        ch.inc(i)
        for kt in range(nkt):
            ch.wait(s)
            ch.inc(s.activation(out=sq_sb[:, 0:tcols], in_=slabs[kt],
                                func=AF.Square))
            ch.wait(t)
            ch.inc(t.matmul(ps_b[0:1, 0:tcols], ones_sb[:, 0:1],
                            sq_sb[:, 0:tcols],
                            start=(kt == 0), stop=(kt == nkt - 1)))
            if kt < nkt - 1:
                ch.wait(s)
    ch.wait(v)
    inv_n = 1.0 / float(nfeat)
    mu = mu_row[0:1, 0:tcols]
    rs = rstd_row[0:1, 0:tcols]
    sc = scr_row[0:1, 0:tcols]
    ch.inc(v.tensor_scalar(mu, ps_a[0:1, 0:tcols], inv_n, None, ALU.mult))
    ch.wait(v)
    ch.inc(v.tensor_tensor(sc, mu, mu, ALU.mult))
    ch.wait(v)
    ch.inc(v.scalar_tensor_tensor(rs, ps_b[0:1, 0:tcols], inv_n, sc, ALU.mult,
                                  ALU.subtract))
    ch.wait(s)
    ch.inc(s.activation(out=sc, in_=rs, func=AF.Sqrt, bias=eps_ap))
    ch.wait(v)
    ch.inc(v.reciprocal(rs, sc))


def bcast_rows(ph, ch, rows_and_ps, tcols, ones_row):
    t = ph.eng["tensor"]
    ch.wait(t)
    for row, ps in rows_and_ps:
        i = t.matmul(ps[:, 0:tcols], ones_row[0:1, 0:128], row[0:1, 0:tcols],
                     start=True, stop=True)
    ch.inc(i)



def pack_vec(v, nblk):
    return np.ascontiguousarray(v.reshape(nblk, 128).T).astype(np.float32)




import contextlib
import numpy as np
import concourse.bass as bass
import concourse.mybir as mybir



def build_fused(nc, T=8192, TC=512):
    NTC = T // TC
    NKT = 8
    HB = 22
    HBW = [128] * 21 + [42]
    NW = T // 512
    ph = Phase(nc)
    dram = {}
    def din(name, shape, dt=F32):
        dram[name] = nc.dram_tensor(name, shape, dt, kind="ExternalInput").ap()
        return dram[name]
    def dscr(name, shape, dt=F32):
        dram[name] = nc.dram_tensor(name, shape, dt).ap()
        return dram[name]

    xin = din("xin", [1024, T])
    w_in = din("w_in", [1024, 4096]); w_xp = din("w_xp", [2048, 96])
    w_dt = din("w_dt", [64, 2048])
    cwd = din("cwd", [128, 64]); cbd = din("cbd", [128, 16])
    bdtd = din("bdtd", [128, 16]); alogd = din("alogd", [32768])
    ed = din("ed", [128, 2048]); rd = din("rd", [128, 2048], BF16)
    rbcd = din("rbcd", [16, 128]); dskd = din("dskd", [128, 16])
    woutd = din("woutd", [2048, 1024])
    wqkv_l = [din("wqkv0", [1024, 1536]), din("wqkv1", [1024, 1536])]
    wo_l = [din("wo0", [512, 1024]), din("wo1", [512, 1024])]
    qsd_l = [din("qsd0", [128, 2]), din("qsd1", [128, 2])]
    tabqd = din("tabqd", [128, 2 * T], BF16)
    tabkd = din("tabkd", [128, 2 * T], BF16)
    mgend = din("mgend", [128, 8 * 512], BF16)
    identd = din("identd", [128, 128], BF16)
    rotd = din("rotd", [128, 128]); bo2d = din("bo2d", [128, 2])
    bc2d = din("bc2d", [2, 128])
    fw1_l = [din("fw1_0", [1024, 5460]), din("fw1_1", [1024, 5460])]
    fw2_l = [din("fw2_0", [2730, 1024]), din("fw2_1", [2730, 1024])]
    fb1a_l = [din("fb1a0", [128, HB]), din("fb1a1", [128, HB])]
    fb1g_l = [din("fb1g0", [128, HB]), din("fb1g1", [128, HB])]
    flg_l = [din("flg0", [128, 8]), din("flg1", [128, 8])]
    flb_l = [din("flb0", [128, 8]), din("flb1", [128, 8])]
    fb2_l = [din("fb2_0", [128, 8]), din("fb2_1", [128, 8])]
    seld = din("seld", [128, 2048])
    wkTd = din("wkTd", [1024, 1024]); wvTd = din("wvTd", [1024, 1024])
    wrTd = din("wrTd", [1024, 1024]); woTd = din("woTd", [1024, 1024])
    tmd = din("tmd", [128, 8]); tdd = din("tdd", [128, 8]); tfd = din("tfd", [128, 8])
    lw1 = din("lw1", [1024, 4096]); lw2 = din("lw2", [4096, 1024])
    lb1d = din("lb1d", [128, 32]); lb2d = din("lb2d", [128, 8])
    lgmd = din("lgmd", [128, 8]); lbtd = din("lbtd", [128, 8])
    onesd = din("onesd", [128, 128]); constsd = din("constsd", [1, 8])
    o_x4 = nc.dram_tensor("o_x4", [1024, T], F32, kind="ExternalOutput").ap()

    s_d = dscr("s_d", [2048, T], BF16); s_u = dscr("s_u", [2048, T])
    s_du = dscr("s_du", [2048, T], BF16)
    s_z = dscr("s_z", [2048, T])
    s_B = dscr("s_B", [16, T], BF16); s_C = dscr("s_C", [16, T], BF16)
    s_m = dscr("s_m", [1024, T])
    s_x1 = dscr("s_x1", [1024, T]); s_xp = dscr("s_xp", [1024, T])
    s_xa = dscr("s_xa", [1024, T])
    s_qf = dscr("s_qf", [512, T], BF16); s_kf = dscr("s_kf", [512, T], BF16)
    s_vt = dscr("s_vt", [T, 512], BF16)
    s_x2 = dscr("s_x2", [1024, T]); s_x3 = dscr("s_x3", [1024, T])

    esg = contextlib.ExitStack()
    P = [esg.enter_context(nc.psum_tensor(f"ps{i}", [128, 512], F32))
         for i in range(8)]
    par_sb = esg.enter_context(nc.sbuf_tensor("par_g", [128, 430], F32))
    ones_row = esg.enter_context(nc.sbuf_tensor("ones_row", [1, 128], F32))
    rows = esg.enter_context(nc.sbuf_tensor("rows_g", [1, 2048], F32))
    rows2 = esg.enter_context(nc.sbuf_tensor("rows2_g", [2, 512], F32))
    scrA = esg.enter_context(nc.sbuf_tensor("scrA", [128, 512], F32))
    scrB = esg.enter_context(nc.sbuf_tensor("scrB", [128, 512], F32))
    scrC = esg.enter_context(nc.sbuf_tensor("scrC", [128, 512], F32))
    lnbf = esg.enter_context(nc.sbuf_tensor("lnbf_g", [128, 2048], BF16))
    onesb = esg.enter_context(nc.sbuf_tensor("onesb_g", [128, 1], BF16))
    ones_col = par_sb[:, 0:1]
    eps_ap = par_sb[0:1, 1:2]
    mu1, rs1 = rows[0:1, 0:512], rows[0:1, 512:1024]
    scrow = rows[0:1, 1024:1536]

    ld = DQ(ph, "sync", "ld")
    sdq = DQ(ph, "scalar", "sdq")
    tdq = DQ(ph, "tensor", "tdq")
    wld = [DQ(ph, "gpsimd", "wldA"), DQ(ph, "gpsimd", "wldB")]
    st = DQ(ph, "gpsimd", "st")
    st2 = DQ(ph, "sync", "st2")
    ch = Chain(ph, "ch")
    chS = Chain(ph, "chS")
    chV = Chain(ph, "chV")
    mm = Chain(ph, "mm")
    t, v, s, g, sy = (ph.eng["tensor"], ph.eng["vector"], ph.eng["scalar"],
                      ph.eng["gpsimd"], ph.eng["sync"])

    ld.start(ones_row[0:1, :], onesd[0:1, :], serialize=False)
    ld.start(ones_col, onesd[:, 0:1], serialize=False,
             allow_slow_non_contiguous=True)
    ld.start(eps_ap, constsd[0:1, 0:1], serialize=False,
             allow_slow_non_contiguous=True)
    wld[0].start(onesb[:, 0:1], onesd[:, 0:1], serialize=False,
                 allow_slow_non_contiguous=True)
    for e in ph.eng.values():
        ld.wait_all(e); wld[0].wait_all(e)

    kt3 = lambda buf, nkt: buf[:, :].rearrange("p (kt t) -> p kt t", kt=nkt)
    dr3 = lambda ap: ap.rearrange("(kt p) t -> p kt t", p=128)

    def stage_sync():
        for e in ph.eng.values():
            ld.wait_all(e); st.wait_all(e); st2.wait_all(e)
            wld[0].wait_all(e); wld[1].wait_all(e)
            sdq.wait_all(e); tdq.wait_all(e)
            e.wait_ge(ch.sem, ch.n)
            e.wait_ge(chS.sem, chS.n)
            e.wait_ge(chV.sem, chV.n)
            e.wait_ge(mm.sem, mm.n)
        ph.barrier()

    def ln_block(slabs, tcols, out_fn):
        ch.wait(t); ch.wait(s)
        ln_stats(ph, ch, slabs, tcols, P[6], P[7], scrA, ones_col,
                 mu1, rs1, scrow, eps_ap=eps_ap)
        bcast_rows(ph, ch, [(mu1, P[6]), (rs1, P[7])], tcols, ones_row)
        ch.wait(v)
        for kt in range(len(slabs)):
            out_fn(kt)

    with contextlib.ExitStack() as es:
        E = es.enter_context
        x_sb = E(nc.sbuf_tensor("m1_x", [128, NKT * TC], F32))
        lnx_bf = E(nc.sbuf_tensor("m1_lnx", [128, NKT * TC], BF16))
        win_sb = E(nc.sbuf_tensor("m1_win", [128, NKT * 4096], BF16))
        wxp_sb = E(nc.sbuf_tensor("m1_wxp", [128, 16 * 96], BF16))
        wdt_sb = E(nc.sbuf_tensor("m1_wdt", [64, 16 * 128], BF16))
        u_bf = E(nc.sbuf_tensor("m1_ubf", [128, 16 * TC], BF16))
        xin2 = E(nc.sbuf_tensor("m1_xin2", [128, 2 * (TC + 3)], F32))
        carry_sb = E(nc.sbuf_tensor("m1_car", [128, 48], F32))
        cscr = E(nc.sbuf_tensor("m1_cscr", [128, 2 * TC], F32))
        zstg = E(nc.sbuf_tensor("m1_zstg", [128, 2 * TC], F32))
        u_fs = E(nc.sbuf_tensor("m1_ufs", [128, 16 * TC], F32))
        dscr = E(nc.sbuf_tensor("m1_dscr", [128, 6 * TC], F32))
        xdbl_sb = E(nc.sbuf_tensor("m1_xdbl", [96, TC], F32))
        dtl_bf = E(nc.sbuf_tensor("m1_dtl", [64, TC], BF16))
        pp = E(nc.sbuf_tensor("m1_pp", [128, 100], F32))
        cw_sb, cb_sb, bdt_sb = pp[:, 0:64], pp[:, 64:80], pp[:, 80:96]

        wld[0].start(kt3(win_sb, NKT), dr3(w_in), serialize=False)
        wld[0].start(kt3(wxp_sb, 16), dr3(w_xp), serialize=False)
        wld[0].start(wdt_sb[:, :].rearrange("kk (kt o) -> kk kt o", kt=16),
                     w_dt.rearrange("kk (kt o) -> kk kt o", kt=16),
                     serialize=False)
        ld.start(cw_sb, cwd[:, :], serialize=False)
        ld.start(cb_sb, cbd[:, :], serialize=False)
        ld.start(bdt_sb, bdtd[:, :], serialize=False)
        for e in (t, v, s, g, sy):
            ld.wait_all(e); wld[0].wait_all(e)
        ch.wait(v)
        ch.inc(v.memset(carry_sb[:, :], 0.0))
        ph.barrier()

        for c in range(NTC):
            t0 = c * TC
            sy.wait_ge(ch.sem, ch.n); sy.wait_ge(mm.sem, mm.n)
            ld.start(kt3(x_sb, NKT), dr3(xin)[:, :, t0:t0 + TC])
            for e in (t, v, s):
                ld.wait_all(e); st.wait_all(e); st2.wait_all(e)
            xkt = lambda kt: x_sb[:, kt*TC:(kt+1)*TC]

            def apply_ln1(kt):
                ch.wait(v)
                ch.inc(v.tensor_tensor(scrA[:, 0:TC], xkt(kt), P[6][:, 0:TC],
                                       ALU.subtract))
                ch.wait(v)
                ch.inc(v.tensor_tensor(lnx_bf[:, kt*TC:(kt+1)*TC],
                                       scrA[:, 0:TC], P[7][:, 0:TC], ALU.mult))
            ln_block([xkt(kt) for kt in range(NKT)], TC, apply_ln1)
            lnx_ready = ch.n
            ph.barrier()

            ev_u = {}; conv_done = {}; silu_done = {}; ubf_done = {}
            for ob in range(16):
                par = ob % 2
                ps = P[0] if par == 0 else P[4]
                xv = xin2[:, par*(TC+3): par*(TC+3)+TC+3]
                sc = cscr[:, par*TC:(par+1)*TC]
                t.wait_ge(ch.sem, lnx_ready)
                if ob >= 2:
                    t.wait_ge(chS.sem, ev_u[ob-2])
                for kt in range(NKT):
                    i = t.matmul(ps[:, 0:TC],
                                 win_sb[:, kt*4096 + ob*128: kt*4096 + (ob+1)*128],
                                 lnx_bf[:, kt*TC:(kt+1)*TC],
                                 start=(kt == 0), stop=(kt == NKT-1))
                mm.inc(i)
                s.wait_ge(mm.sem, mm.n)
                if ob >= 2:
                    s.wait_ge(chV.sem, conv_done[ob-2])
                chS.inc(s.activation(out=xv[:, 3:3+TC], in_=ps[:, 0:TC],
                                     func=AF.Identity))
                ev_u[ob] = chS.n
                chV.inc(v.tensor_copy(xv[:, 0:3], carry_sb[:, ob*3:(ob+1)*3]))
                v.wait_ge(chS.sem, ev_u[ob])
                if ob >= 2:
                    v.wait_ge(chS.sem, silu_done[ob-2])
                chV.inc(v.tensor_scalar(sc[:, 0:TC], xv[:, 0:TC],
                                        cw_sb[:, 4*ob:4*ob+1], None, ALU.mult))
                for kk in range(1, 4):
                    chV.inc(v.scalar_tensor_tensor(sc[:, 0:TC],
                                                   xv[:, kk:kk+TC],
                                                   cw_sb[:, 4*ob+kk:4*ob+kk+1],
                                                   sc[:, 0:TC], ALU.mult, ALU.add))
                chV.inc(v.tensor_copy(carry_sb[:, ob*3:(ob+1)*3],
                                      xv[:, TC:TC+3]))
                conv_done[ob] = chV.n
                s.wait_ge(chV.sem, conv_done[ob])
                chS.inc(s.activation(out=u_fs[:, ob*TC:(ob+1)*TC],
                                     in_=sc[:, 0:TC],
                                     func=AF.Silu, bias=cb_sb[:, ob:ob+1]))
                silu_done[ob] = chS.n
                v.wait_ge(chS.sem, silu_done[ob])
                chV.inc(v.tensor_copy(u_bf[:, ob*TC:(ob+1)*TC],
                                      u_fs[:, ob*TC:(ob+1)*TC]))
                ubf_done[ob] = chV.n
                g.wait_ge(chS.sem, silu_done[ob])
                st.start(s_u[ob*128:(ob+1)*128, t0:t0+TC],
                         u_fs[:, ob*TC:(ob+1)*TC])
            ev_z = {}; st2n = {}
            for ob in range(16):
                par = ob % 2
                ps = P[1] if par == 0 else P[5]
                zs = zstg[:, par*TC:(par+1)*TC]
                if ob >= 2:
                    t.wait_ge(chS.sem, ev_z[ob-2])
                for kt in range(NKT):
                    i = t.matmul(ps[:, 0:TC],
                                 win_sb[:, kt*4096 + (16+ob)*128:
                                        kt*4096 + (17+ob)*128],
                                 lnx_bf[:, kt*TC:(kt+1)*TC],
                                 start=(kt == 0), stop=(kt == NKT-1))
                mm.inc(i)
                s.wait_ge(mm.sem, mm.n)
                if ob >= 2:
                    s.wait_ge(st2.sem, st2n[ob-2])
                chS.inc(s.activation(out=zs, in_=ps[:, 0:TC], func=AF.Identity))
                ev_z[ob] = chS.n
                sy.wait_ge(chS.sem, ev_z[ob])
                st2.start(s_z[ob*128:(ob+1)*128, t0:t0+TC], zs)
                st2n[ob] = st2.n
            t.wait_ge(chV.sem, ubf_done[15])
            for kt in range(16):
                i = t.matmul(P[2][0:96, 0:TC], wxp_sb[:, kt*96:(kt+1)*96],
                             u_bf[:, kt*TC:(kt+1)*TC],
                             start=(kt == 0), stop=(kt == 15))
            mm.inc(i)
            s.wait_ge(mm.sem, mm.n)
            chS.inc(s.activation(out=xdbl_sb[:, 0:TC], in_=P[2][0:96, 0:TC],
                                 func=AF.Identity))
            xdbl_ev = chS.n
            g.wait_ge(chS.sem, xdbl_ev)
            st.start(s_B[:, t0:t0+TC], xdbl_sb[64:80, 0:TC])
            st.start(s_C[:, t0:t0+TC], xdbl_sb[80:96, 0:TC])
            v.wait_ge(chS.sem, xdbl_ev)
            chV.inc(v.tensor_copy(dtl_bf[:, 0:TC], xdbl_sb[0:64, 0:TC]))
            dtl_ready = chV.n
            ev_d = {}; add_d = {}; stn_d = {}
            for ob in range(16):
                par = ob % 2
                ps = P[3] if par == 0 else P[2]
                sA = dscr[:, (par*3+0)*TC:(par*3+1)*TC]
                sB = dscr[:, (par*3+1)*TC:(par*3+2)*TC]
                sC = dscr[:, (par*3+2)*TC:(par*3+3)*TC]
                t.wait_ge(chV.sem, dtl_ready)
                if ob >= 2:
                    t.wait_ge(chS.sem, ev_d[ob-2])
                mm.inc(t.matmul(ps[:, 0:TC], wdt_sb[:, ob*128:(ob+1)*128],
                                dtl_bf[:, 0:TC], start=True, stop=True))
                s.wait_ge(mm.sem, mm.n)
                if ob >= 2:
                    s.wait_ge(chV.sem, add_d[ob-2])
                    s.wait_ge(st.sem, stn_d[ob-2])
                chS.inc(s.activation(out=sA, in_=ps[:, 0:TC],
                                     func=AF.Abs, bias=bdt_sb[:, ob:ob+1]))
                chS.inc(s.activation(out=sA, in_=sA, func=AF.Exp, scale=-1.0))
                chS.inc(s.activation(out=sA, in_=sA, func=AF.Ln,
                                     bias=ones_col[:, 0:1]))
                chS.inc(s.activation(out=sB, in_=ps[:, 0:TC],
                                     func=AF.Relu, bias=bdt_sb[:, ob:ob+1]))
                ev_d[ob] = chS.n
                v.wait_ge(chS.sem, ev_d[ob])
                if ob >= 2:
                    v.wait_ge(st.sem, stn_d[ob-2])
                chV.inc(v.tensor_tensor(sB, sA, sB, ALU.add))
                chV.inc(v.tensor_tensor(sC, sB, u_fs[:, ob*TC:(ob+1)*TC],
                                        ALU.mult))
                add_d[ob] = chV.n
                g.wait_ge(chV.sem, add_d[ob])
                st.start(s_d[ob*128:(ob+1)*128, t0:t0+TC], sB)
                st.start(s_du[ob*128:(ob+1)*128, t0:t0+TC], sC)
                stn_d[ob] = st.n
            ph.barrier()
        stage_sync()

    TC2 = 1024
    NTC2 = T // TC2
    with contextlib.ExitStack() as es:
        E = es.enter_context
        z1_sb = E(nc.sbuf_tensor("m2_z1", [128, TC2], F32))
        u1_sb = E(nc.sbuf_tensor("m2_u1", [128, TC2], F32))
        bhat_sb = E(nc.sbuf_tensor("m2_bh", [128, TC2], BF16))
        chat_sb = E(nc.sbuf_tensor("m2_ch", [128, TC2], BF16))
        dA_sb = E(nc.sbuf_tensor("m2_dA", [128, 6 * TC2], BF16))
        dBu_sb = E(nc.sbuf_tensor("m2_dBu", [128, 6 * TC2], BF16))
        h_sb = E(nc.sbuf_tensor("m2_h", [128, 2 * TC2], BF16))
        hc_sb = E(nc.sbuf_tensor("m2_hc", [128, 2 * TC2], BF16))
        y2_bf = E(nc.sbuf_tensor("m2_y2", [128, 16 * TC2], BF16))
        szz_sb = E(nc.sbuf_tensor("m2_szz", [128, TC2], F32))
        t1_sb = E(nc.sbuf_tensor("m2_t1", [128, TC2], F32))
        stg = E(nc.sbuf_tensor("m2_stg", [128, 2 * TC2], F32))
        hcar_sb = E(nc.sbuf_tensor("m2_hcar", [128, 256], F32))
        r_sb = E(nc.sbuf_tensor("m2_r", [128, 2048], BF16))
        sel_sb = E(nc.sbuf_tensor("m2_sel", [128, 2048], BF16))
        dtb_sb = E(nc.sbuf_tensor("m2_dtb", [128, 2 * TC2], BF16))
        dub_sb = E(nc.sbuf_tensor("m2_dub", [128, 2 * TC2], BF16))
        av_sb = E(nc.sbuf_tensor("m2_av", [128, 272], F32))
        wo_sb = E(nc.sbuf_tensor("m2_wo", [128, 16 * 1024], BF16))
        ds_sb = av_sb[:, 256:272]

        ld.start(r_sb[:, :], rd[:, :], serialize=False)
        wld[0].start(sel_sb[:, :], seld[:, :], serialize=False)
        ld.start(av_sb[:, 0:256], alogd.rearrange("(st p) -> p st", p=128),
                 serialize=False, allow_slow_non_contiguous=True)
        ld.start(ds_sb, dskd[:, :], serialize=False)
        wld[0].start(kt3(wo_sb, 16), dr3(woutd), serialize=False)
        for e in (t, v, s, g, sy):
            ld.wait_all(e); wld[0].wait_all(e)
        ch.wait(s)
        ch.inc(s.activation(out=av_sb[:, 0:256], in_=av_sb[:, 0:256], func=AF.Exp))
        ch.wait(v)
        ch.inc(v.tensor_scalar(av_sb[:, 0:256], av_sb[:, 0:256], -1.0, None,
                               ALU.mult))
        ch.wait(v)
        ch.inc(v.memset(hcar_sb[:, :], 0.0))
        ph.barrier()

        for c in range(NTC2):
            t0 = c * TC2
            sy.wait_ge(ch.sem, ch.n)
            sy.wait_ge(chS.sem, chS.n)
            sy.wait_ge(chV.sem, chV.n)
            sy.wait_ge(mm.sem, mm.n)
            ld.start(bhat_sb[:, :],
                     s_B[0:16, t0:t0+TC2].unsqueeze(0).broadcast_to([8, 16, TC2]))
            sdq.start(chat_sb[:, :],
                      s_C[0:16, t0:t0+TC2].unsqueeze(0).broadcast_to([8, 16, TC2]))
            for e in (t, v, s):
                ld.wait_all(e); sdq.wait_all(e); st.wait_all(e)

            def epilogue(dt_i):
                py = (P[0], P[1])
                sy.wait_ge(chS.sem, chS.n)
                ld.start(z1_sb[:, :], s_z[dt_i*128:(dt_i+1)*128, t0:t0+TC2])
                sy.wait_ge(chV.sem, chV.n)
                ld.start(u1_sb[:, :], s_u[dt_i*128:(dt_i+1)*128, t0:t0+TC2])
                ld.wait_all(s)
                chS.inc(s.activation(out=szz_sb[:, 0:TC2], in_=z1_sb[:, 0:TC2],
                                     func=AF.Silu))
                v.wait_ge(mm.sem, mm.n)
                v.wait_ge(chS.sem, chS.n)
                ld.wait_all(v)
                for hf in range(2):
                    v.scalar_tensor_tensor(t1_sb[:, hf*512:(hf+1)*512],
                                           u1_sb[:, hf*512:(hf+1)*512],
                                           ds_sb[:, dt_i:dt_i+1],
                                           py[hf][:, 0:512], ALU.mult, ALU.add)
                chV.inc(v.tensor_tensor(y2_bf[:, dt_i*TC2:(dt_i+1)*TC2],
                                        t1_sb[:, 0:TC2], szz_sb[:, 0:TC2],
                                        ALU.mult))

            exp_done = {}; v_done = {}; y_done = {}
            selB_done = {}; scan_evs = {}
            NSLOT = 6
            dt_ldn = {}
            for pre in range(2):
                ld.start(dub_sb[:, pre*TC2:(pre+1)*TC2],
                         s_du[pre*128:(pre+1)*128, t0:t0+TC2])
                dt_ldn[pre] = ld.n
            QS = [ld, sdq, st2]
            need_a = {}
            issued = -1
            def issue_dA(si):
                slot = si % NSLOT
                qa = QS[si % 3]
                if si >= NSLOT:
                    qa.e.wait_ge(chV.sem, scan_evs[si - NSLOT])
                src_a = s_d[si*8:si*8 + 8, t0:t0+TC2].unsqueeze(1
                            ).broadcast_to([8, 16, TC2])
                qa.start(dA_sb[:, slot*TC2:(slot+1)*TC2], src_a)
                need_a[si] = (qa, qa.n)
            def emit_du(si):
                dt_j = si // 16
                st_j = si % 16
                psBj = (P[4], P[5]) if si % 2 == 0 else (P[2], P[3])
                selj = sel_sb[:, st_j*128:(st_j+1)*128]
                dubj = dub_sb[:, (dt_j % 2)*TC2:(dt_j % 2 + 1)*TC2]
                if st_j == 0:
                    t.wait_ge(ld.sem, dt_ldn[dt_j])
                if si >= 2:
                    t.wait_ge(chV.sem, scan_evs[si-2])
                for hf in range(2):
                    mm.inc(t.matmul(psBj[hf][:, 0:512], selj,
                                    dubj[:, hf*512:(hf+1)*512],
                                    start=True, stop=True))
                selB_done[si] = mm.n
            emit_du(0)
            for sti in range(256):
                dt_i = sti // 16
                st16 = sti % 16
                slot = sti % NSLOT
                seg_dA = dA_sb[:, slot*TC2:(slot+1)*TC2]
                seg_du = dBu_sb[:, slot*TC2:(slot+1)*TC2]
                par = sti % 2
                segh = lambda buf: buf[:, par*TC2:(par+1)*TC2]
                while issued < min(255, sti + NSLOT - 1):
                    issued += 1
                    issue_dA(issued)
                if st16 == 0 and 1 <= dt_i < 15:
                    nxt = dt_i + 1
                    sy.wait_ge(mm.sem, selB_done[(dt_i-1)*16 + 15])
                    ld.start(dub_sb[:, (nxt % 2)*TC2:(nxt % 2 + 1)*TC2],
                             s_du[nxt*128:(nxt+1)*128, t0:t0+TC2])
                    dt_ldn[nxt] = ld.n
                qa, na = need_a[sti]
                s.wait_ge(qa.sem, na)
                chS.inc(s.activation(out=seg_dA, in_=seg_dA,
                                     func=AF.Exp, scale=av_sb[:, sti:sti+1]))
                exp_done[sti] = chS.n
                v.wait_ge(mm.sem, selB_done[sti])
                v.wait_ge(chS.sem, exp_done[sti])
                if sti >= 2:
                    v.wait_ge(chV.sem, v_done[sti-2])
                psBc = (P[4], P[5]) if par == 0 else (P[2], P[3])
                for hf in range(2):
                    v.scalar_tensor_tensor(seg_du[:, hf*512:(hf+1)*512],
                                           psBc[hf][:, 0:512], 1.0,
                                           bhat_sb[:, hf*512:(hf+1)*512],
                                           ALU.mult, ALU.mult)
                chV.inc(v.tensor_tensor_scan(segh(h_sb), seg_dA, seg_du,
                                             hcar_sb[:, sti:sti+1], ALU.mult,
                                             ALU.add))
                scan_evs[sti] = chV.n
                v.tensor_copy(hcar_sb[:, sti:sti+1], segh(h_sb)[:, TC2-1:TC2])
                g.wait_ge(chV.sem, scan_evs[sti])
                if sti >= 2:
                    g.wait_ge(mm.sem, y_done[sti-2])
                chV.inc(g.tensor_tensor(segh(hc_sb), segh(h_sb),
                                        chat_sb[:, 0:TC2], ALU.mult))
                v_done[sti] = chV.n
                if sti + 1 < 256:
                    emit_du(sti + 1)
                t.wait_ge(chV.sem, v_done[sti])
                py = (P[0], P[1])
                for hf in range(2):
                    i = t.matmul(py[hf][:, 0:512],
                                 r_sb[:, st16*128:(st16+1)*128],
                                 hc_sb[:, par*TC2 + hf*512: par*TC2 + (hf+1)*512],
                                 start=(st16 == 0), stop=(st16 == 15))
                mm.inc(i)
                y_done[sti] = mm.n
                if st16 == 15:
                    epilogue(dt_i)
            y2_ready = chV.n
            add_done = {}
            for ob in range(8):
                for hf in range(2):
                    idx = ob * 2 + hf
                    psw = P[4] if idx % 2 == 0 else P[5]
                    t.wait_ge(chV.sem, y2_ready)
                    if idx >= 2:
                        t.wait_ge(chS.sem, add_done[idx-2])
                    for dt_i in range(16):
                        i = t.matmul(psw[:, 0:512],
                                     wo_sb[:, dt_i*1024 + ob*128:
                                           dt_i*1024 + (ob+1)*128],
                                     y2_bf[:, dt_i*TC2 + hf*512:
                                           dt_i*TC2 + (hf+1)*512],
                                     start=(dt_i == 0), stop=(dt_i == 15))
                    mm.inc(i)
                    s.wait_ge(mm.sem, mm.n)
                    s.wait_ge(st.sem, max(0, st.n - 16))
                    chS.inc(s.activation(out=stg[:, (idx % 2)*TC2//1:
                                             (idx % 2)*TC2//1 + 512]
                                         if False else
                                         stg[:, (idx % 2)*1024:(idx % 2)*1024+512],
                                         in_=psw[:, 0:512], func=AF.Identity))
                    add_done[idx] = chS.n
                    g.wait_ge(chS.sem, chS.n)
                    st.start(s_m[ob*128:(ob+1)*128, t0 + hf*512: t0 + (hf+1)*512],
                             stg[:, (idx % 2)*1024:(idx % 2)*1024+512])
            ph.barrier()
        stage_sync()

    with contextlib.ExitStack() as es:
        E = es.enter_context
        mslab = E(nc.sbuf_tensor("s3_m", [128, NKT * TC], F32))
        xslab = E(nc.sbuf_tensor("s3_x", [128, NKT * TC], F32))
        for c in range(NTC):
            t0 = c * TC
            sy.wait_ge(ch.sem, ch.n)
            st.wait_all(sy); st2.wait_all(sy)
            ld.start(kt3(mslab, NKT), dr3(s_m)[:, :, t0:t0+TC])
            ld.start(kt3(xslab, NKT), dr3(xin)[:, :, t0:t0+TC])
            for e in (t, v, s):
                ld.wait_all(e); st.wait_all(e); st2.wait_all(e)
            mkt = lambda kt: mslab[:, kt*TC:(kt+1)*TC]
            xkt = lambda kt: xslab[:, kt*TC:(kt+1)*TC]

            def apply1(kt):
                ch.wait(v)
                ch.inc(v.tensor_tensor(scrA[:, 0:TC], mkt(kt), P[6][:, 0:TC],
                                       ALU.subtract))
                ch.wait(v)
                ch.inc(v.tensor_tensor(scrB[:, 0:TC], scrA[:, 0:TC],
                                       P[7][:, 0:TC], ALU.mult))
                ch.wait(v)
                ch.inc(v.tensor_tensor(mkt(kt), scrB[:, 0:TC], xkt(kt), ALU.add))
            ln_block([mkt(kt) for kt in range(NKT)], TC, apply1)
            g.wait_ge(ch.sem, ch.n)
            st.start(dr3(s_x1)[:, :, t0:t0+TC], kt3(mslab, NKT))

            def apply2(kt):
                ch.wait(v)
                ch.inc(v.tensor_tensor(scrA[:, 0:TC], mkt(kt), P[6][:, 0:TC],
                                       ALU.subtract))
                ch.wait(v)
                ch.inc(v.tensor_tensor(xslab[:, kt*TC:(kt+1)*TC], scrA[:, 0:TC],
                                       P[7][:, 0:TC], ALU.mult))
            ch.wait(t); ch.wait(s)
            ln_stats(ph, ch, [mkt(kt) for kt in range(NKT)], TC, P[6], P[7],
                     scrA, ones_col, mu1, rs1, scrow, eps_ap=eps_ap)
            bcast_rows(ph, ch, [(mu1, P[6]), (rs1, P[7])], TC, ones_row)
            ch.wait(v)
            for kt in range(NKT):
                apply2(kt)
            sy.wait_ge(ch.sem, ch.n)
            st2.start(dr3(s_xp)[:, :, t0:t0+TC], kt3(xslab, NKT))
            ph.barrier()
        stage_sync()

    def attn_stage(layer, src_xp, dst_xa):
        wqkv, wo, qsd = wqkv_l[layer], wo_l[layer], qsd_l[layer]
        with contextlib.ExitStack() as es:
            E = es.enter_context
            xp_sb = E(nc.sbuf_tensor(f"a{layer}_xp", [128, NKT * (TC + 1)], F32))
            xp_bf = E(nc.sbuf_tensor(f"a{layer}_xpb", [128, NKT * (TC + 1)], BF16))
            wq_sb = E(nc.sbuf_tensor(f"a{layer}_wq", [128, NKT * 1536], BF16))
            wo_bf = E(nc.sbuf_tensor(f"a{layer}_wo", [128, 4 * 1024], BF16))
            tabq_sb = E(nc.sbuf_tensor(f"a{layer}_tq", [128, 2 * T], BF16))
            tabk_sb = E(nc.sbuf_tensor(f"a{layer}_tk", [128, 2 * T], BF16))
            mg_sb = E(nc.sbuf_tensor(f"a{layer}_mg", [128, 8 * 512], BF16))
            id_sb = E(nc.sbuf_tensor(f"a{layer}_id", [128, 128], BF16))
            rot_sb = E(nc.sbuf_tensor(f"a{layer}_rot", [128, 128], F32))
            bo2_sb = E(nc.sbuf_tensor(f"a{layer}_bo2", [128, 2], F32))
            bc2_sb = E(nc.sbuf_tensor(f"a{layer}_bc2", [2, 128], F32))
            qf_sb = E(nc.sbuf_tensor(f"a{layer}_qf", [128, 4 * 512], BF16))
            kf_sb = E(nc.sbuf_tensor(f"a{layer}_kf", [128, 4 * 1024], BF16))
            vtok_sb = E(nc.sbuf_tensor(f"a{layer}_vt", [128, 8 * 512], BF16))
            P_bf = E(nc.sbuf_tensor(f"a{layer}_P", [128, 2 * 512], BF16))
            PT_bf = E(nc.sbuf_tensor(f"a{layer}_PT", [128, 8 * 128], BF16))
            otok_bf = E(nc.sbuf_tensor(f"a{layer}_ot", [128, 512], BF16))
            ofw_bf = E(nc.sbuf_tensor(f"a{layer}_of", [128, 4 * 512], BF16))
            rinv_sb = E(nc.sbuf_tensor(f"a{layer}_ri", [128, 16], F32))
            qs_sb = E(nc.sbuf_tensor(f"a{layer}_qs", [128, 2], F32))
            xres = E(nc.sbuf_tensor(f"a{layer}_xr", [128, NKT * 512], F32))

            wld[0].start(kt3(wo_bf, 4), dr3(wo), serialize=False)
            wld[1].start(kt3(wq_sb, NKT), dr3(wqkv), serialize=False)
            ld.start(tabq_sb[:, :], tabqd[:, :], serialize=False)
            ld.start(tabk_sb[:, :], tabkd[:, :], serialize=False)
            ld.start(mg_sb[:, :], mgend[:, :], serialize=False)
            ld.start(id_sb[:, :], identd[:, :], serialize=False)
            ld.start(rot_sb[:, :], rotd[:, :], serialize=False)
            ld.start(bo2_sb[:, :], bo2d[:, :], serialize=False)
            ld.start(bc2_sb[:, :], bc2d[:, :], serialize=False)
            ld.start(qs_sb[:, :], qsd[:, :], serialize=False)
            for e in ph.eng.values():
                ld.wait_all(e); wld[0].wait_all(e)
            ph.barrier()

            def qknorm(c5, is_q, psrc):
                s.wait_ge(mm.sem, mm.n)
                ch.wait(s)
                ch.inc(s.activation(out=scrA[:, 0:512], in_=psrc[:, 0:512],
                                    func=AF.Identity))
                ch.wait(s)
                ch.inc(s.activation(out=scrB[:, 0:512], in_=scrA[:, 0:512],
                                    func=AF.Square))
                ch.wait(t)
                mm.inc(t.matmul(P[6][0:2, 0:512], bo2_sb[:, 0:2],
                                scrB[:, 0:512], start=True, stop=True))
                s.wait_ge(mm.sem, mm.n)
                ch.wait(s)
                ch.inc(s.activation(out=rows2[0:2, 0:512], in_=P[6][0:2, 0:512],
                                    func=AF.Sqrt))
                ch.wait(v)
                ch.inc(v.tensor_scalar(rows2[0:2, 0:512], rows2[0:2, 0:512],
                                       1e-12, None, ALU.max))
                ch.wait(v)
                ch.inc(v.reciprocal(rows2[0:2, 0:512], rows2[0:2, 0:512]))
                ch.wait(t)
                mm.inc(t.matmul(P[7][:, 0:512], bc2_sb[:, :],
                                rows2[0:2, 0:512], start=True, stop=True))
                v.wait_ge(mm.sem, mm.n)
                ch.wait(v)
                ch.inc(v.scalar_tensor_tensor(
                    scrB[:, 0:512], scrA[:, 0:512],
                    qs_sb[:, 0:1] if is_q else qs_sb[:, 1:2],
                    P[7][:, 0:512], ALU.mult, ALU.mult))
                ch.wait(t)
                mm.inc(t.matmul(P[6][:, 0:512], rot_sb[:, :], scrB[:, 0:512],
                                start=True, stop=True))
                tab = tabq_sb if is_q else tabk_sb
                tcos = tab[:, c5*512:(c5+1)*512]
                tsin = tab[:, T + c5*512: T + (c5+1)*512]
                ch.wait(v)
                ch.inc(v.tensor_tensor(scrC[:, 0:512], scrB[:, 0:512], tcos,
                                       ALU.mult))
                v.wait_ge(mm.sem, mm.n)
                ch.wait(v)
                ch.inc(v.scalar_tensor_tensor(scrB[:, 0:512], P[6][:, 0:512],
                                              1.0, tsin, ALU.mult, ALU.mult))
                ch.wait(v)

            for c5 in range(NTC):
                xc0 = c5 * TC
                sy.wait_ge(ch.sem, ch.n); sy.wait_ge(mm.sem, mm.n)
                ld.start(kt3(xp_sb, NKT)[:, :, 1:TC+1],
                         dr3(src_xp)[:, :, xc0:xc0+TC])
                if c5 == 0:
                    ch.wait(v)
                    ch.inc(v.memset(kt3(xp_sb, NKT)[:, :, 0:1], 0.0))
                else:
                    ld.start(kt3(xp_sb, NKT)[:, :, 0:1],
                             dr3(src_xp)[:, :, xc0-1:xc0],
                             allow_slow_non_contiguous=True)
                for e in (t, v, s):
                    ld.wait_all(e); st.wait_all(e); st2.wait_all(e)
                for kt in range(NKT):
                    ch.wait(v)
                    ch.inc(v.tensor_copy(xp_bf[:, kt*(TC+1):(kt+1)*(TC+1)],
                                         xp_sb[:, kt*(TC+1):(kt+1)*(TC+1)]))
                xpb_ready = ch.n
                for ob in range(12):
                    t.wait_ge(ch.sem, ch.n)
                    t.wait_ge(chS.sem, chS.n)
                    t.wait_ge(chV.sem, chV.n)
                    for kt in range(NKT):
                        off = 1 if kt < 4 else 0
                        i = t.matmul(P[0][:, 0:TC],
                                     wq_sb[:, kt*1536 + ob*128:
                                           kt*1536 + (ob+1)*128],
                                     xp_bf[:, kt*(TC+1) + off: kt*(TC+1) + off + TC],
                                     start=(kt == 0), stop=(kt == NKT-1))
                    mm.inc(i)
                    if ob < 4:
                        qknorm(c5, True, P[0])
                        ch.inc(v.tensor_tensor(
                            qf_sb[:, ob*512:(ob+1)*512], scrC[:, 0:512],
                            scrB[:, 0:512], ALU.add))
                        g.wait_ge(ch.sem, ch.n)
                        st.start(s_qf[ob*128:(ob+1)*128, xc0:xc0+TC],
                                 qf_sb[:, ob*512:(ob+1)*512])
                    elif ob < 8:
                        qknorm(c5, False, P[0])
                        ch.inc(v.tensor_tensor(
                            kf_sb[:, (ob-4)*1024:(ob-4)*1024+512],
                            scrC[:, 0:512], scrB[:, 0:512], ALU.add))
                        g.wait_ge(ch.sem, ch.n)
                        st.start(s_kf[(ob-4)*128:(ob-3)*128, xc0:xc0+TC],
                                 kf_sb[:, (ob-4)*1024:(ob-4)*1024+512])
                    else:
                        s.wait_ge(mm.sem, mm.n)
                        ch.wait(s)
                        ch.inc(s.activation(out=P_bf[:, 0:512],
                                            in_=P[0][:, 0:TC], func=AF.Identity))
                        ch.wait(t)
                        for js in range(4):
                            t.wait_ge(ch.sem, ch.n)
                            i = t.transpose(P[1].bitcast(BF16)[:, 0:128],
                                            P_bf[:, js*128:(js+1)*128],
                                            id_sb[:, :])
                            mm.inc(i)
                            s.wait_ge(mm.sem, mm.n)
                            ch.wait(s)
                            ch.inc(s.activation(
                                out=vtok_sb[:, js*512 + (ob-8)*128:
                                            js*512 + (ob-7)*128],
                                in_=P[1].bitcast(BF16)[:, 0:128],
                                func=AF.Identity))
                        if ob == 11:
                            g.wait_ge(ch.sem, ch.n)
                            st.start(s_vt[xc0: xc0 + 512, :].rearrange(
                                         "(js p) e -> p js e", p=128),
                                     vtok_sb[:, 0:4*512].rearrange(
                                         "p (js e) -> p js e", js=4))
                ph.barrier()

            for wq in range(NW):
                sy.wait_ge(ch.sem, ch.n); sy.wait_ge(mm.sem, mm.n)
                st.wait_all(sy); st2.wait_all(sy)
                ld.start(kt3(qf_sb, 4), s_qf.rearrange(
                             "(kt p) t -> p kt t", p=128)[:, :, wq*512:(wq+1)*512])
                klo = max(0, (wq - 1) * 512)
                ld.start(kt3(kf_sb, 4)[:, :, 0:(wq+1)*512 - klo],
                         s_kf.rearrange("(kt p) t -> p kt t", p=128)[
                             :, :, klo:(wq+1)*512])
                njt = ((wq + 1) * 512 - klo) // 128
                ld.start(vtok_sb[:, :].rearrange("p (jt e) -> p jt e", jt=8)[
                             :, 0:njt, :],
                         s_vt[klo:(wq+1)*512, :].rearrange(
                             "(jt p) e -> p jt e", p=128))
                for e in (t, v, s):
                    ld.wait_all(e); st.wait_all(e); st2.wait_all(e)
                nj = ((wq + 1) * 512 - klo) // 128
                nchunks = 1 if wq == 0 else 2
                for bi in range(4):
                    for h in range(8):
                        hb = 64 * (h % 2)
                        ht = h // 2
                        t.wait_ge(ch.sem, ch.n)
                        lq = qf_sb[hb:hb+64, ht*512 + bi*128: ht*512 + (bi+1)*128]
                        for cc in range(nchunks):
                            mm.inc(t.matmul(
                                P[cc][:, 0:512], lq,
                                kf_sb[hb:hb+64, ht*1024 + cc*512: ht*1024 + (cc+1)*512],
                                start=True, stop=True))
                        v.wait_ge(mm.sem, mm.n)
                        for cc in range(nchunks):
                            mi = (bi*2 + (cc + (2 - nchunks)))
                            ch.wait(v)
                            ch.inc(v.tensor_tensor(
                                P[cc][:, 0:512], P[cc][:, 0:512],
                                mg_sb[:, mi*512:(mi+1)*512], ALU.add))
                        for cc in range(nchunks):
                            ch.wait(v)
                            ch.inc(v.reduce_max(scrA[:, cc:cc+1], P[cc][:, 0:512],
                                                axis=mybir.AxisListType.X))
                        if nchunks == 2:
                            ch.wait(v)
                            ch.inc(v.tensor_tensor(scrA[:, 2:3], scrA[:, 0:1],
                                                   scrA[:, 1:2], ALU.max))
                            mx = scrA[:, 2:3]
                        else:
                            mx = scrA[:, 0:1]
                        ch.wait(v)
                        ch.inc(v.tensor_scalar(scrA[:, 3:4], mx, -1.0, None,
                                               ALU.mult))
                        s.wait_ge(mm.sem, mm.n)
                        for cc in range(nchunks):
                            ch.wait(s)
                            ch.inc(s.activation(out=P_bf[:, cc*512:(cc+1)*512],
                                                in_=P[cc][:, 0:512], func=AF.Exp,
                                                bias=scrA[:, 3:4],
                                                accum_out=scrA[:, 4+cc:5+cc]))
                        if nchunks == 2:
                            ch.wait(v)
                            ch.inc(v.tensor_tensor(scrA[:, 6:7], scrA[:, 4:5],
                                                   scrA[:, 5:6], ALU.add))
                            sm = scrA[:, 6:7]
                        else:
                            sm = scrA[:, 4:5]
                        ch.wait(v)
                        ch.inc(v.reciprocal(rinv_sb[:, h:h+1], sm))
                        ch.wait(t)
                        pt_ev = {}
                        for js in range(nj):
                            pspt = P[2] if js % 2 == 0 else P[7]
                            if js >= 2:
                                t.wait_ge(chS.sem, pt_ev[js-2])
                            i = t.transpose(pspt.bitcast(BF16)[:, 0:128],
                                            P_bf[:, js*128:(js+1)*128],
                                            id_sb[:, :])
                            mm.inc(i)
                            s.wait_ge(mm.sem, mm.n)
                            chS.inc(s.activation(out=PT_bf[:, js*128:(js+1)*128],
                                                 in_=pspt.bitcast(BF16)[:, 0:128],
                                                 func=AF.Identity))
                            pt_ev[js] = chS.n
                        t.wait_ge(chS.sem, chS.n)
                        for js in range(nj):
                            i = t.matmul(P[4][:, 64*h:64*h+64],
                                         PT_bf[:, js*128:(js+1)*128],
                                         vtok_sb[:, js*512 + 64*h: js*512 + 64*h + 64],
                                         start=(js == 0), stop=(js == nj-1))
                        mm.inc(i)
                    v.wait_ge(mm.sem, mm.n)
                    for h in range(8):
                        ch.wait(v)
                        ch.inc(v.tensor_scalar(otok_bf[:, 64*h:64*h+64],
                                               P[4][:, 64*h:64*h+64],
                                               rinv_sb[:, h:h+1], None, ALU.mult))
                    ch.wait(t)
                    for et in range(4):
                        t.wait_ge(ch.sem, ch.n)
                        i = t.transpose(P[3].bitcast(BF16)[:, 0:128],
                                        otok_bf[:, et*128:(et+1)*128], id_sb[:, :])
                        mm.inc(i)
                        s.wait_ge(mm.sem, mm.n)
                        ch.wait(s)
                        ch.inc(s.activation(out=ofw_bf[:, et*512 + bi*128:
                                                       et*512 + (bi+1)*128],
                                            in_=P[3].bitcast(BF16)[:, 0:128],
                                            func=AF.Identity))
                sy.wait_ge(ch.sem, ch.n)
                st.wait_all(sy)
                ld.start(kt3(xres, NKT), dr3(src_xp)[:, :, wq*512:(wq+1)*512])
                for e in (t, v, s):
                    ld.wait_all(e)
                t.wait_ge(ch.sem, ch.n)
                add_done = {}
                for ob in range(NKT):
                    par = ob % 2
                    psw = P[5] if par == 0 else P[6]
                    if ob >= 2:
                        t.wait_ge(ch.sem, add_done[ob-2])
                    for et in range(4):
                        i = t.matmul(psw[:, 0:512],
                                     wo_bf[:, et*1024 + ob*128: et*1024 + (ob+1)*128],
                                     ofw_bf[:, et*512:(et+1)*512],
                                     start=(et == 0), stop=(et == 3))
                    mm.inc(i)
                    v.wait_ge(mm.sem, mm.n)
                    v.wait_ge(st.sem, max(0, st.n - 16))
                    ch.wait(v)
                    ch.inc(v.tensor_tensor(xres[:, ob*512:(ob+1)*512],
                                           psw[:, 0:512],
                                           xres[:, ob*512:(ob+1)*512], ALU.add))
                    add_done[ob] = ch.n
                    g.wait_ge(ch.sem, ch.n)
                    st.start(dr3(dst_xa)[:, ob, wq*512:(wq+1)*512],
                             xres[:, ob*512:(ob+1)*512])
                ph.barrier()
            stage_sync()

    def geglu_stage(layer, src_xa, dst, last):
        w1, w2 = fw1_l[layer], fw2_l[layer]
        with contextlib.ExitStack() as es:
            E = es.enter_context
            xa_sb = E(nc.sbuf_tensor(f"g{layer}_xa", [128, NKT * (TC + 1)], F32))
            x1_sb = E(nc.sbuf_tensor(f"g{layer}_x1", [128, NKT * TC], F32))
            tgt_bf = E(nc.sbuf_tensor(f"g{layer}_tg", [128, NKT * TC], BF16))
            h_bf = E(nc.sbuf_tensor(f"g{layer}_h", [128, HB * TC], BF16))
            out_sb = E(nc.sbuf_tensor(f"g{layer}_o", [128, NKT * TC], F32))
            w1_sb = E(nc.sbuf_tensor(f"g{layer}_w1", [128, NKT * 5460], BF16))
            w2buf = E(nc.sbuf_tensor(f"g{layer}_w2", [128, 2 * HB * 128], BF16))
            pp = E(nc.sbuf_tensor(f"g{layer}_pp", [128, 2 * HB + 24], F32))
            b1a_sb = pp[:, 0:HB]
            b1g_sb = pp[:, HB:2*HB]
            lg_sb = pp[:, 2*HB:2*HB+8]
            lb_sb = pp[:, 2*HB+8:2*HB+16]
            b2_sb = pp[:, 2*HB+16:2*HB+24]
            wld[1].start(kt3(w1_sb, NKT), dr3(w1), serialize=False)
            ld.start(b1a_sb, fb1a_l[layer][:, :], serialize=False)
            ld.start(b1g_sb, fb1g_l[layer][:, :], serialize=False)
            ld.start(lg_sb, flg_l[layer][:, :], serialize=False)
            ld.start(lb_sb, flb_l[layer][:, :], serialize=False)
            ld.start(b2_sb, fb2_l[layer][:, :], serialize=False)
            for e in ph.eng.values():
                ld.wait_all(e); wld[1].wait_all(e)
            ph.barrier()

            for c in range(NTC):
                t0 = c * TC
                sy.wait_ge(ch.sem, ch.n); sy.wait_ge(mm.sem, mm.n)
                sy.wait_ge(chS.sem, chS.n); sy.wait_ge(chV.sem, chV.n)
                st.wait_all(sy)
                ld.start(kt3(xa_sb, NKT)[:, :, 1:TC+1],
                         dr3(src_xa)[:, :, t0:t0+TC])
                if c == 0:
                    ch.wait(v)
                    ch.inc(v.memset(kt3(xa_sb, NKT)[:, :, 0:1], 0.0))
                else:
                    ld.start(kt3(xa_sb, NKT)[:, :, 0:1],
                             dr3(src_xa)[:, :, t0-1:t0],
                             allow_slow_non_contiguous=True)
                if last:
                    ld.start(kt3(x1_sb, NKT), dr3(s_x1)[:, :, t0:t0+TC])
                for e in (t, v, s):
                    ld.wait_all(e); st.wait_all(e)
                def xsh(kt):
                    off = 1 if kt < 4 else 0
                    return xa_sb[:, kt*(TC+1)+off: kt*(TC+1)+off+TC]
                def xcur(kt):
                    return xa_sb[:, kt*(TC+1)+1: kt*(TC+1)+1+TC]
                ch.wait(t); ch.wait(s)
                ln_stats(ph, ch, [xsh(kt) for kt in range(NKT)], TC, P[6], P[7],
                         scrA, ones_col, mu1, rs1, scrow, eps_ap=eps_ap)
                bcast_rows(ph, ch, [(mu1, P[6]), (rs1, P[7])], TC, ones_row)
                ch.wait(v)
                act_done = {}
                for kt in range(NKT):
                    scr_o = scrB if kt % 2 == 0 else scrC
                    ch.wait(v)
                    if kt >= 2:
                        v.wait_ge(ch.sem, act_done[kt-2])
                    ch.inc(v.tensor_tensor(scrA[:, 0:TC], xsh(kt), P[6][:, 0:TC],
                                           ALU.subtract))
                    ch.wait(v)
                    ch.inc(v.tensor_tensor(scr_o[:, 0:TC], scrA[:, 0:TC],
                                           P[7][:, 0:TC], ALU.mult))
                    s.wait_ge(ch.sem, ch.n)
                    ch.inc(s.activation(out=tgt_bf[:, kt*TC:(kt+1)*TC],
                                        in_=scr_o[:, 0:TC], func=AF.Identity,
                                        scale=lg_sb[:, kt:kt+1],
                                        bias=lb_sb[:, kt:kt+1]))
                    act_done[kt] = ch.n
                tgt_ready = ch.n
                h_done = {}
                s_done = {}
                for hb in range(HB):
                    slot = hb % 2
                    bw = HBW[hb]
                    ps_a = P[0] if slot == 0 else P[2]
                    ps_g = P[1] if slot == 0 else P[3]
                    t.wait_ge(ch.sem, tgt_ready)
                    if hb >= 2:
                        t.wait_ge(chS.sem, s_done[hb-2])
                    for kt in range(NKT):
                        t.matmul(ps_a[0:bw, 0:TC],
                                 w1_sb[:, kt*5460 + hb*128: kt*5460 + hb*128 + bw],
                                 tgt_bf[:, kt*TC:(kt+1)*TC],
                                 start=(kt == 0), stop=(kt == NKT-1))
                    for kt in range(NKT):
                        i = t.matmul(ps_g[0:bw, 0:TC],
                                     w1_sb[:, kt*5460 + 2730 + hb*128:
                                           kt*5460 + 2730 + hb*128 + bw],
                                     tgt_bf[:, kt*TC:(kt+1)*TC],
                                     start=(kt == 0), stop=(kt == NKT-1))
                    mm.inc(i)
                    s.wait_ge(mm.sem, mm.n)
                    if hb >= 1:
                        s.wait_ge(chV.sem, h_done[hb-1])
                    s.activation(out=scrA[0:bw, 0:TC], in_=ps_a[0:bw, 0:TC],
                                 func=AF.Identity, bias=b1a_sb[0:bw, hb:hb+1])
                    chS.inc(s.activation(out=scrB[0:bw, 0:TC],
                                         in_=ps_g[0:bw, 0:TC],
                                         func=AF.Gelu, bias=b1g_sb[0:bw, hb:hb+1]))
                    s_done[hb] = chS.n
                    v.wait_ge(chS.sem, chS.n)
                    chV.inc(v.tensor_tensor(h_bf[0:bw, hb*TC:(hb+1)*TC],
                                            scrA[0:bw, 0:TC], scrB[0:bw, 0:TC],
                                            ALU.mult))
                    h_done[hb] = chV.n
                h_ready = chV.n
                add_done = {}
                w2_base = [wld[0].n, wld[1].n]
                mm_base_w2 = mm.n
                for ob in range(NKT):
                    slot = ob % 2
                    if ob >= 2:
                        g.wait_ge(mm.sem, mm_base_w2 + (ob - 1) * HB)
                    wslab = w2buf[:, slot*HB*128:(slot+1)*HB*128]
                    wld[slot].start(
                        wslab.rearrange("p (kt o) -> p kt o", kt=HB)[:, 0:21, :],
                        w2[0:2688, ob*128:(ob+1)*128].rearrange(
                            "(kt p) o -> p kt o", p=128))
                    wld[slot].start(
                        wslab.rearrange("p (kt o) -> p kt o", kt=HB)[0:42, 21:22, :],
                        w2[2688:2730, ob*128:(ob+1)*128].rearrange(
                            "(kt p) o -> p kt o", p=42))
                for ob in range(NKT):
                    slot = ob % 2
                    par = ob % 2
                    psw = P[4] if par == 0 else P[5]
                    t.wait_ge(chV.sem, h_ready)
                    t.wait_ge(wld[slot].sem, w2_base[slot] + 32*(ob//2 + 1))
                    if ob >= 2:
                        t.wait_ge(chS.sem, add_done[ob-2])
                    wslab = w2buf[:, slot*HB*128:(slot+1)*HB*128]
                    for hb in range(HB):
                        bw = HBW[hb]
                        mm.inc(t.matmul(psw[:, 0:TC],
                                        wslab[0:bw, hb*128:(hb+1)*128],
                                        h_bf[0:bw, hb*TC:(hb+1)*TC],
                                        start=(hb == 0), stop=(hb == HB-1)))
                    s.wait_ge(mm.sem, mm.n)
                    chS.inc(s.activation(out=scrA[:, 0:TC], in_=psw[:, 0:TC],
                                         func=AF.Identity, bias=b2_sb[:, ob:ob+1]))
                    v.wait_ge(chS.sem, chS.n)
                    chV.inc(v.tensor_tensor(out_sb[:, ob*TC:(ob+1)*TC],
                                            scrA[:, 0:TC], xcur(ob), ALU.add))
                    add_done[ob] = chS.n
                if last:
                    for e_ in (t, v, s):
                        e_.wait_ge(chV.sem, chV.n)
                        e_.wait_ge(chS.sem, chS.n)
                    def apply_last(kt):
                        ch.wait(v)
                        ch.inc(v.tensor_tensor(scrA[:, 0:TC],
                                               out_sb[:, kt*TC:(kt+1)*TC],
                                               P[6][:, 0:TC], ALU.subtract))
                        ch.wait(v)
                        ch.inc(v.tensor_tensor(scrB[:, 0:TC], scrA[:, 0:TC],
                                               P[7][:, 0:TC], ALU.mult))
                        ch.wait(v)
                        ch.inc(v.tensor_tensor(out_sb[:, kt*TC:(kt+1)*TC],
                                               scrB[:, 0:TC],
                                               x1_sb[:, kt*TC:(kt+1)*TC], ALU.add))
                    ln_block([out_sb[:, kt*TC:(kt+1)*TC] for kt in range(NKT)],
                             TC, apply_last)
                g.wait_ge(ch.sem, ch.n)
                g.wait_ge(chV.sem, chV.n)
                st.start(dr3(dst)[:, :, t0:t0+TC], kt3(out_sb, NKT))
                ph.barrier()
            stage_sync()

    def rwkv_stage(src_x2, dst_x3):
        with contextlib.ExitStack() as es:
            E = es.enter_context
            x2_sb = E(nc.sbuf_tensor("r_x2", [128, NKT * TC], F32))
            xrs_sb = E(nc.sbuf_tensor("r_xrs", [128, NKT * (TC + 1)], F32))
            xm_bf = E(nc.sbuf_tensor("r_xm", [128, NKT * TC], BF16))
            wk_sb = E(nc.sbuf_tensor("r_wk", [128, NKT * 1024], BF16))
            wv_sb = E(nc.sbuf_tensor("r_wv", [128, NKT * 1024], BF16))
            wr_sb = E(nc.sbuf_tensor("r_wr", [128, NKT * 1024], BF16))
            wobuf = E(nc.sbuf_tensor("r_wob", [128, 2 * NKT * 128], BF16))
            sr1_sb = E(nc.sbuf_tensor("r_sr1", [128, TC], F32))
            ks_sb = E(nc.sbuf_tensor("r_ks", [128, NKT * (TC + 1)], F32))
            kvs_sb = E(nc.sbuf_tensor("r_kvs", [128, NKT * (TC + 1)], F32))
            sk_sb = E(nc.sbuf_tensor("r_sk", [128, NKT * (TC + 1)], F32))
            skv_sb = E(nc.sbuf_tensor("r_skv", [128, NKT * (TC + 1)], F32))
            rw_bf = E(nc.sbuf_tensor("r_rw", [128, NKT * TC], BF16))
            ro_sb = E(nc.sbuf_tensor("r_ro", [128, NKT * TC], F32))
            pp = E(nc.sbuf_tensor("r_pp", [128, 40], F32))
            tm_sb = pp[:, 0:8]
            tm1_sb = pp[:, 8:16]
            dec_sb = pp[:, 16:24]
            tfx_sb = pp[:, 24:32]
            ld.start(tm_sb, tmd[:, :], serialize=False)
            ld.start(dec_sb, tdd[:, :], serialize=False)
            ld.start(tfx_sb, tfd[:, :], serialize=False)
            wld[0].start(kt3(wk_sb, NKT), dr3(wkTd), serialize=False)
            wld[0].start(kt3(wv_sb, NKT), dr3(wvTd), serialize=False)
            wld[0].start(kt3(wr_sb, NKT), dr3(wrTd), serialize=False)
            for e in ph.eng.values():
                ld.wait_all(e); wld[0].wait_all(e)
            ch.wait(v)
            ch.inc(v.tensor_scalar(tm1_sb, tm_sb, -1.0, 1.0, ALU.mult, ALU.add))
            ch.wait(s)
            ch.inc(s.activation(out=dec_sb, in_=dec_sb, func=AF.Exp))
            ch.wait(s)
            ch.inc(s.activation(out=dec_sb, in_=dec_sb, func=AF.Exp, scale=-1.0))
            ch.wait(s)
            ch.inc(s.activation(out=tfx_sb, in_=tfx_sb, func=AF.Exp))
            for buf in (ks_sb, kvs_sb, sk_sb, skv_sb, xrs_sb):
                ch.wait(v)
                ch.inc(v.memset(buf[:, :].rearrange(
                    "p (kt t) -> p kt t", kt=NKT)[:, :, 0:1], 0.0))
            ph.barrier()

            for c in range(NTC):
                t0 = c * TC
                sy.wait_ge(ch.sem, ch.n); sy.wait_ge(mm.sem, mm.n)
                st.wait_all(sy)
                ld.start(kt3(x2_sb, NKT), dr3(src_x2)[:, :, t0:t0+TC])
                for e in (t, v, s):
                    ld.wait_all(e); st.wait_all(e)
                x2kt = lambda kt: x2_sb[:, kt*TC:(kt+1)*TC]

                def applyr(kt):
                    ch.wait(v)
                    ch.inc(v.tensor_tensor(scrA[:, 0:TC], x2kt(kt), P[6][:, 0:TC],
                                           ALU.subtract))
                    ch.wait(v)
                    ch.inc(v.tensor_tensor(xrs_sb[:, kt*(TC+1)+1: kt*(TC+1)+1+TC],
                                           scrA[:, 0:TC], P[7][:, 0:TC], ALU.mult))
                ln_block([x2kt(kt) for kt in range(NKT)], TC, applyr)
                for kt in range(NKT):
                    ch.wait(v)
                    ch.inc(v.tensor_scalar(scrA[:, 0:TC],
                                           xrs_sb[:, kt*(TC+1)+1: kt*(TC+1)+1+TC],
                                           tm_sb[:, kt:kt+1], None, ALU.mult))
                    ch.wait(v)
                    ch.inc(v.scalar_tensor_tensor(xm_bf[:, kt*TC:(kt+1)*TC],
                                                  xrs_sb[:, kt*(TC+1): kt*(TC+1)+TC],
                                                  tm1_sb[:, kt:kt+1], scrA[:, 0:TC],
                                                  ALU.mult, ALU.add))
                ch.wait(v)
                ch.inc(v.tensor_copy(
                    kt3(xrs_sb, NKT)[:, :, 0:1], kt3(xrs_sb, NKT)[:, :, TC:TC+1]))
                xm_ready = ch.n
                for ob in range(NKT):
                    t.wait_ge(ch.sem, ch.n)
                    for kt in range(NKT):
                        t.matmul(P[0][:, 0:TC],
                                 wk_sb[:, kt*1024 + ob*128: kt*1024 + (ob+1)*128],
                                 xm_bf[:, kt*TC:(kt+1)*TC], start=(kt == 0),
                                 stop=(kt == NKT-1))
                    for kt in range(NKT):
                        t.matmul(P[1][:, 0:TC],
                                 wv_sb[:, kt*1024 + ob*128: kt*1024 + (ob+1)*128],
                                 xm_bf[:, kt*TC:(kt+1)*TC], start=(kt == 0),
                                 stop=(kt == NKT-1))
                    for kt in range(NKT):
                        i = t.matmul(P[2][:, 0:TC],
                                     wr_sb[:, kt*1024 + ob*128: kt*1024 + (ob+1)*128],
                                     xm_bf[:, kt*TC:(kt+1)*TC], start=(kt == 0),
                                     stop=(kt == NKT-1))
                    mm.inc(i)
                    v.wait_ge(mm.sem, mm.n)
                    kt = ob
                    ch.wait(v)
                    ch.inc(v.tensor_scalar(scrA[:, 0:TC], P[0][:, 0:TC],
                                           60.0, None, ALU.min))
                    ch.wait(s)
                    ch.inc(s.activation(out=ks_sb[:, kt*(TC+1)+1: kt*(TC+1)+1+TC],
                                        in_=scrA[:, 0:TC], func=AF.Exp))
                    ch.wait(s)
                    ch.inc(s.activation(out=scrB[:, 0:TC], in_=P[1][:, 0:TC],
                                        func=AF.Identity))
                    ch.wait(s)
                    ch.inc(s.activation(out=sr1_sb[:, 0:TC],
                                        in_=P[2][:, 0:TC], func=AF.Sigmoid))
                    ch.wait(v)
                    ch.inc(v.tensor_tensor(kvs_sb[:, kt*(TC+1)+1: kt*(TC+1)+1+TC],
                                           ks_sb[:, kt*(TC+1)+1: kt*(TC+1)+1+TC],
                                           scrB[:, 0:TC], ALU.mult))
                    d0 = dec_sb[:, kt:kt+1].broadcast_to([128, TC])
                    ch.wait(v)
                    ch.inc(v.tensor_tensor_scan(
                        sk_sb[:, kt*(TC+1)+1: kt*(TC+1)+1+TC], d0,
                        ks_sb[:, kt*(TC+1): kt*(TC+1)+TC],
                        sk_sb[:, kt*(TC+1): kt*(TC+1)+1], ALU.mult, ALU.add))
                    ch.wait(v)
                    ch.inc(v.tensor_tensor_scan(
                        skv_sb[:, kt*(TC+1)+1: kt*(TC+1)+1+TC], d0,
                        kvs_sb[:, kt*(TC+1): kt*(TC+1)+TC],
                        skv_sb[:, kt*(TC+1): kt*(TC+1)+1], ALU.mult, ALU.add))
                    ch.wait(v)
                    ch.inc(v.tensor_copy(sk_sb[:, kt*(TC+1): kt*(TC+1)+1],
                                         sk_sb[:, kt*(TC+1)+TC: kt*(TC+1)+TC+1]))
                    ch.wait(v)
                    ch.inc(v.tensor_copy(skv_sb[:, kt*(TC+1): kt*(TC+1)+1],
                                         skv_sb[:, kt*(TC+1)+TC: kt*(TC+1)+TC+1]))
                    ch.wait(v)
                    ch.inc(v.tensor_copy(ks_sb[:, kt*(TC+1): kt*(TC+1)+1],
                                         ks_sb[:, kt*(TC+1)+TC: kt*(TC+1)+TC+1]))
                    ch.wait(v)
                    ch.inc(v.tensor_copy(kvs_sb[:, kt*(TC+1): kt*(TC+1)+1],
                                         kvs_sb[:, kt*(TC+1)+TC: kt*(TC+1)+TC+1]))
                    ch.wait(v)
                    ch.inc(v.scalar_tensor_tensor(
                        skv_sb[:, kt*(TC+1)+1: kt*(TC+1)+1+TC],
                        kvs_sb[:, kt*(TC+1)+1: kt*(TC+1)+1+TC],
                        tfx_sb[:, kt:kt+1],
                        skv_sb[:, kt*(TC+1)+1: kt*(TC+1)+1+TC],
                        ALU.mult, ALU.add))
                    ch.wait(v)
                    ch.inc(v.scalar_tensor_tensor(
                        scrA[:, 0:TC],
                        ks_sb[:, kt*(TC+1)+1: kt*(TC+1)+1+TC],
                        tfx_sb[:, kt:kt+1],
                        sk_sb[:, kt*(TC+1)+1: kt*(TC+1)+1+TC],
                        ALU.mult, ALU.add))
                    ch.wait(v)
                    ch.inc(v.tensor_scalar(scrA[:, 0:TC], scrA[:, 0:TC], 1e-16,
                                           None, ALU.add))
                    ch.wait(v)
                    ch.inc(v.reciprocal(scrA[:, 0:TC], scrA[:, 0:TC]))
                    ch.wait(v)
                    ch.inc(v.tensor_tensor(scrB[:, 0:TC],
                                           skv_sb[:, kt*(TC+1)+1: kt*(TC+1)+1+TC],
                                           scrA[:, 0:TC], ALU.mult))
                    ch.wait(v)
                    ch.inc(v.tensor_tensor(rw_bf[:, kt*TC:(kt+1)*TC],
                                           scrB[:, 0:TC],
                                           sr1_sb[:, 0:TC], ALU.mult))
                rw_ready = ch.n
                mm_base_wo = mm.n
                for ob in range(NKT):
                    slot = ob % 2
                    if ob >= 2:
                        g.wait_ge(mm.sem, mm_base_wo + ob - 1)
                    wld[slot].start(
                        wobuf[:, slot*NKT*128:(slot+1)*NKT*128].rearrange(
                            "p (kt o) -> p kt o", kt=NKT),
                        dr3(woTd)[:, :, ob*128:(ob+1)*128])
                wo_base = [wld[0].n - (NKT//2)*16, wld[1].n - (NKT//2)*16]
                evd = {}
                for ob in range(NKT):
                    slot = ob % 2
                    t.wait_ge(ch.sem, rw_ready)
                    if ob >= 1:
                        t.wait_ge(ch.sem, evd[ob-1])
                    t.wait_ge(wld[slot].sem, wo_base[slot] + 16*(ob//2 + 1))
                    for kt in range(NKT):
                        i = t.matmul(P[3][:, 0:TC],
                                     wobuf[:, slot*NKT*128 + kt*128:
                                           slot*NKT*128 + (kt+1)*128],
                                     rw_bf[:, kt*TC:(kt+1)*TC], start=(kt == 0),
                                     stop=(kt == NKT-1))
                    mm.inc(i)
                    s.wait_ge(mm.sem, mm.n)
                    ch.wait(s)
                    ch.inc(s.activation(out=ro_sb[:, ob*TC:(ob+1)*TC],
                                        in_=P[3][:, 0:TC], func=AF.Identity))
                    evd[ob] = ch.n

                def applyro(kt):
                    ch.wait(v)
                    ch.inc(v.tensor_tensor(scrA[:, 0:TC],
                                           ro_sb[:, kt*TC:(kt+1)*TC],
                                           P[6][:, 0:TC], ALU.subtract))
                    ch.wait(v)
                    ch.inc(v.tensor_tensor(scrB[:, 0:TC], scrA[:, 0:TC],
                                           P[7][:, 0:TC], ALU.mult))
                    ch.wait(v)
                    ch.inc(v.tensor_tensor(ro_sb[:, kt*TC:(kt+1)*TC],
                                           scrB[:, 0:TC], x2kt(kt), ALU.add))
                ln_block([ro_sb[:, kt*TC:(kt+1)*TC] for kt in range(NKT)], TC,
                         applyro)
                g.wait_ge(ch.sem, ch.n)
                st.start(dr3(dst_x3)[:, :, t0:t0+TC], kt3(ro_sb, NKT))
                ph.barrier()
            stage_sync()

    def l1ffn_stage(src_x3, dst):
        HB32 = 32
        with contextlib.ExitStack() as es:
            E = es.enter_context
            x3_sb = E(nc.sbuf_tensor("l_x3", [128, NKT * TC], F32))
            tgt_bf = E(nc.sbuf_tensor("l_tg", [128, NKT * TC], BF16))
            h_bf = E(nc.sbuf_tensor("l_h", [128, HB32 * TC], BF16))
            eo_sb = E(nc.sbuf_tensor("l_eo", [128, NKT * TC], F32))
            xi_sb = E(nc.sbuf_tensor("l_xi", [128, NKT * TC], F32))
            w1_sb = E(nc.sbuf_tensor("l_w1", [128, NKT * 4096], BF16))
            w2buf = E(nc.sbuf_tensor("l_w2", [128, 2 * 512], BF16))
            pp = E(nc.sbuf_tensor("l_pp", [128, 60], F32))
            b1_sb = pp[:, 0:32]
            b2_sb = pp[:, 32:40]
            gm_sb = pp[:, 40:48]
            bt_sb = pp[:, 48:56]
            wld[1].start(kt3(w1_sb, NKT), dr3(lw1), serialize=False)
            ld.start(b1_sb, lb1d[:, :], serialize=False)
            ld.start(b2_sb, lb2d[:, :], serialize=False)
            ld.start(gm_sb, lgmd[:, :], serialize=False)
            ld.start(bt_sb, lbtd[:, :], serialize=False)
            for e in ph.eng.values():
                ld.wait_all(e)
            ph.barrier()

            for c in range(NTC):
                t0 = c * TC
                sy.wait_ge(ch.sem, ch.n); sy.wait_ge(mm.sem, mm.n)
                sy.wait_ge(chS.sem, chS.n); sy.wait_ge(chV.sem, chV.n)
                st.wait_all(sy); st2.wait_all(sy)
                ld.start(kt3(x3_sb, NKT), dr3(src_x3)[:, :, t0:t0+TC])
                for e in (t, v, s):
                    ld.wait_all(e); st.wait_all(e); st2.wait_all(e)
                x3kt = lambda kt: x3_sb[:, kt*TC:(kt+1)*TC]

                def applyl1(kt):
                    ch.wait(v)
                    ch.inc(v.tensor_tensor(scrA[:, 0:TC], x3kt(kt), P[6][:, 0:TC],
                                           ALU.subtract))
                    ch.wait(v)
                    ch.inc(v.tensor_tensor(tgt_bf[:, kt*TC:(kt+1)*TC],
                                           scrA[:, 0:TC], P[7][:, 0:TC], ALU.mult))
                ln_block([x3kt(kt) for kt in range(NKT)], TC, applyl1)
                tgt_ready = ch.n
                ch.wait(v)
                ch.inc(v.tensor_copy(rows[0:1, 1536:2048], rows[0:1, 0:512]))
                mu_save = rows[0:1, 1536:2048]
                ch.wait(v)
                ch.inc(v.tensor_copy(rows2[0:1, 0:512], rs1))
                rs_save = rows2[0:1, 0:512]
                h_done = {}
                for hb in range(HB32):
                    slot = hb % 2
                    ps_h = P[0] if slot == 0 else P[1]
                    t.wait_ge(ch.sem, tgt_ready)
                    if hb >= 2:
                        t.wait_ge(chS.sem, h_done[hb-2])
                    for kt in range(NKT):
                        i = t.matmul(ps_h[:, 0:TC],
                                     w1_sb[:, kt*4096 + hb*128:
                                           kt*4096 + (hb+1)*128],
                                     tgt_bf[:, kt*TC:(kt+1)*TC],
                                     start=(kt == 0), stop=(kt == NKT-1))
                    mm.inc(i)
                    s.wait_ge(mm.sem, mm.n)
                    chS.inc(s.activation(out=h_bf[:, hb*TC:(hb+1)*TC],
                                         in_=ps_h[:, 0:TC], func=AF.Relu,
                                         bias=b1_sb[:, hb:hb+1]))
                    h_done[hb] = chS.n
                w2_base = [wld[0].n, wld[1].n]
                mm_base_w2 = mm.n
                for obg in range(2):
                    for kt2 in range(HB32):
                        idx = obg*HB32 + kt2
                        slot = idx % 2
                        if idx >= 2:
                            g.wait_ge(mm.sem, mm_base_w2 + idx - 1)
                        wld[slot].start(w2buf[:, slot*512:(slot+1)*512],
                                        lw2[kt2*128:(kt2+1)*128,
                                            obg*512:(obg+1)*512])
                eo_done = {}
                for obg in range(2):
                    if obg == 1:
                        t.wait_ge(ch.sem, eo_done[0])
                    t.wait_ge(ch.sem, ch.n)
                    t.wait_ge(chS.sem, chS.n)
                    for kt2 in range(HB32):
                        idx = obg*HB32 + kt2
                        slot = idx % 2
                        t.wait_ge(wld[slot].sem, w2_base[slot] + 16*(idx//2 + 1))
                        for ob4 in range(4):
                            i = t.matmul(P[2 + ob4][:, 0:TC],
                                         w2buf[:, slot*512 + ob4*128:
                                               slot*512 + (ob4+1)*128],
                                         h_bf[:, kt2*TC:(kt2+1)*TC],
                                         start=(kt2 == 0), stop=(kt2 == HB32-1))
                        mm.inc(i)
                    s.wait_ge(mm.sem, mm.n)
                    for ob4 in range(4):
                        ko = obg*4 + ob4
                        ch.wait(s)
                        ch.inc(s.activation(out=eo_sb[:, ko*TC:(ko+1)*TC],
                                            in_=P[2 + ob4][:, 0:TC],
                                            func=AF.Identity,
                                            bias=b2_sb[:, ko:ko+1]))
                    eo_done[obg] = ch.n
                bcast_rows(ph, ch, [(mu_save, P[6]), (rs_save, P[7])], TC,
                           ones_row)
                ch.wait(v)
                for kt in range(NKT):
                    ch.wait(v)
                    ch.inc(v.tensor_tensor(scrA[:, 0:TC], x3kt(kt), P[6][:, 0:TC],
                                           ALU.subtract))
                    ch.wait(v)
                    ch.inc(v.tensor_tensor(scrB[:, 0:TC], scrA[:, 0:TC],
                                           P[7][:, 0:TC], ALU.mult))
                    ch.wait(v)
                    ch.inc(v.tensor_tensor(xi_sb[:, kt*TC:(kt+1)*TC],
                                           scrB[:, 0:TC],
                                           eo_sb[:, kt*TC:(kt+1)*TC], ALU.add))
                ch.wait(t); ch.wait(s)
                ln_stats(ph, ch, [xi_sb[:, kt*TC:(kt+1)*TC] for kt in range(NKT)],
                         TC, P[6], P[7], scrA, ones_col, mu1, rs1, scrow,
                         eps_ap=eps_ap)
                bcast_rows(ph, ch, [(mu1, P[6]), (rs1, P[7])], TC, ones_row)
                ch.wait(v)
                act_done = {}
                for kt in range(NKT):
                    scr_o = scrB if kt % 2 == 0 else scrC
                    ch.wait(v)
                    if kt >= 2:
                        v.wait_ge(ch.sem, act_done[kt-2])
                    ch.inc(v.tensor_tensor(scrA[:, 0:TC],
                                           xi_sb[:, kt*TC:(kt+1)*TC],
                                           P[6][:, 0:TC], ALU.subtract))
                    ch.wait(v)
                    ch.inc(v.tensor_tensor(scr_o[:, 0:TC], scrA[:, 0:TC],
                                           P[7][:, 0:TC], ALU.mult))
                    s.wait_ge(ch.sem, ch.n)
                    ch.inc(s.activation(out=eo_sb[:, kt*TC:(kt+1)*TC],
                                        in_=scr_o[:, 0:TC], func=AF.Identity,
                                        scale=gm_sb[:, kt:kt+1],
                                        bias=bt_sb[:, kt:kt+1]))
                    act_done[kt] = ch.n
                def applyl3(kt):
                    ch.wait(v)
                    ch.inc(v.tensor_tensor(scrA[:, 0:TC],
                                           eo_sb[:, kt*TC:(kt+1)*TC],
                                           P[6][:, 0:TC], ALU.subtract))
                    ch.wait(v)
                    ch.inc(v.tensor_tensor(scrB[:, 0:TC], scrA[:, 0:TC],
                                           P[7][:, 0:TC], ALU.mult))
                    ch.wait(v)
                    ch.inc(v.tensor_tensor(xi_sb[:, kt*TC:(kt+1)*TC],
                                           scrB[:, 0:TC], x3kt(kt), ALU.add))
                ch.wait(t); ch.wait(s)
                ln_stats(ph, ch, [eo_sb[:, kt*TC:(kt+1)*TC] for kt in range(NKT)],
                         TC, P[6], P[7], scrA, ones_col, mu1, rs1, scrow,
                         eps_ap=eps_ap)
                bcast_rows(ph, ch, [(mu1, P[6]), (rs1, P[7])], TC, ones_row)
                ch.wait(v)
                for kt in range(NKT):
                    applyl3(kt)
                g.wait_ge(ch.sem, ch.n)
                st.start(dr3(dst)[:, :, t0:t0+TC], kt3(xi_sb, NKT))
                ph.barrier()
            stage_sync()

    attn_stage(0, s_xp, s_xa)
    geglu_stage(0, s_xa, s_xp, last=False)
    attn_stage(1, s_xp, s_xa)
    geglu_stage(1, s_xa, s_x2, last=True)
    rwkv_stage(s_x2, s_x3)
    l1ffn_stage(s_x3, o_x4)
    for e in ph.eng.values():
        st.wait_all(e); st2.wait_all(e)
    ph.close()
    esg.close()
    return nc




import ml_dtypes
from concourse.bass_utils import run_bass_kernel_spmd

_T = 8192
_CORES = list(range(8))


def kernel(**ins):
    x0 = np.asarray(ins["L0_attn_x"], np.float32)
    nc = bass.Bass("TRN2", target_bir_lowering=False)
    build_fused(nc, T=_T, TC=512)

    ones = np.ones((128, 128), np.float32)
    consts = np.array([[1e-5, 0, 0, 0, 0, 0, 0, 0]], np.float32)
    cw = np.zeros((128, 64), np.float32)
    conv_w = ins["L0_attn_conv_w"][:, 0, :]
    for ob in range(16):
        for k in range(4):
            cw[:, 4 * ob + k] = conv_w[ob * 128:(ob + 1) * 128, k]
    E = np.zeros((128, 2048), np.float32)
    R = np.zeros((128, 2048), np.float32)
    for st16 in range(16):
        for p in range(128):
            j = p // 16
            E[8 * st16 + j, st16 * 128 + p] = 1.0
            R[p, st16 * 128 + 8 * st16 + j] = 1.0
    RepBC = np.zeros((16, 128), np.float32)
    for p in range(128):
        RepBC[p % 16, p] = 1.0
    dsk = ins["L0_attn_D_skip"]
    dskd = np.stack([dsk[i * 128:(i + 1) * 128] for i in range(16)], 1
                    ).astype(np.float32)

    half = np.arange(0, 64, 2, dtype=np.float32)
    inv_freq = 1.0 / 10000.0 ** (half / 64)
    base = (half + 0.4 * 64) / (1.4 * 64)
    pos = np.arange(_T, dtype=np.float32)
    fr = pos[:, None] * inv_freq[None, :]
    fr = np.concatenate([fr, fr], -1)
    power = (pos - _T // 2) / 256.0
    sc = base[None, :] ** power[:, None]
    sc = np.concatenate([sc, sc], -1)
    cq, sq_ = np.cos(fr), np.sin(fr)
    cosq8 = (8 * cq * sc).T
    sinq8 = (8 * sq_ * sc).T
    cosk = (cq / sc).T
    sink = (sq_ / sc).T
    tabq = np.concatenate([np.tile(cosq8, (2, 1)).reshape(128, _T),
                           np.tile(sinq8, (2, 1)).reshape(128, _T)],
                          1).astype(ml_dtypes.bfloat16)
    tabk = np.concatenate([np.tile(cosk, (2, 1)).reshape(128, _T),
                           np.tile(sink, (2, 1)).reshape(128, _T)],
                          1).astype(ml_dtypes.bfloat16)
    mg = np.zeros((128, 8 * 512), np.float32)
    jj = np.arange(512)
    for bi in range(4):
        for chunk in range(2):
            blk = np.zeros((128, 512), np.float32)
            for p in range(128):
                i = 128 * bi + p
                j1024 = chunk * 512 + jj
                ok = (j1024 >= i) & (j1024 <= i + 512)
                blk[p] = np.where(ok, 0.0, -1e30)
            mg[:, (bi * 2 + chunk) * 512:(bi * 2 + chunk + 1) * 512] = blk
    mg = mg.astype(ml_dtypes.bfloat16)
    rot = np.zeros((128, 128), np.float32)
    for blk2 in range(2):
        for m_ in range(64):
            if m_ < 32:
                rot[blk2 * 64 + m_ + 32, blk2 * 64 + m_] = -1.0
            else:
                rot[blk2 * 64 + m_ - 32, blk2 * 64 + m_] = 1.0
    bo2 = np.zeros((128, 2), np.float32)
    bc2 = np.zeros((2, 128), np.float32)
    for p in range(128):
        bo2[p, p // 64] = 1.0
        bc2[p // 64, p] = 1.0
    HBW = [128] * 21 + [42]

    def packr(vec):
        out = np.zeros((128, 22), np.float32)
        off = 0
        for i, w in enumerate(HBW):
            out[:w, i] = vec[off:off + w]
            off += w
        return out
    sel_np = np.zeros((128, 2048), np.float32)
    for st in range(16):
        for p in range(128):
            sel_np[st * 8 + p // 16, st * 128 + p] = 1.0

    tm = np.asarray(ins["L1_attn_time_mix"][0, 0], np.float32)
    td_rep = np.repeat(ins["L1_attn_time_decay"][:, 0], 64)
    tf_rep = np.repeat(ins["L1_attn_time_first"][:, 0], 64)

    common = {
        "w_in": ins["L0_attn_W_in"], "w_xp": ins["L0_attn_W_xproj"],
        "w_dt": ins["L0_attn_W_dt"], "cwd": cw,
        "cbd": pack_vec(ins["L0_attn_conv_b"], 16),
        "bdtd": pack_vec(ins["L0_attn_b_dt"], 16),
        "alogd": np.ascontiguousarray(ins["L0_attn_A_log"].reshape(-1)),
        "ed": E, "rd": R.astype(ml_dtypes.bfloat16), "rbcd": RepBC, "dskd": dskd,
        "woutd": ins["L0_attn_W_out"],
        "wqkv0": ins["L0_ffn_wqkv"][0], "wqkv1": ins["L0_ffn_wqkv"][1],
        "wo0": ins["L0_ffn_wo"][0], "wo1": ins["L0_ffn_wo"][1],
        "qsd0": np.stack([np.tile(ins["L0_ffn_q_scale"][0], 2),
                          np.tile(ins["L0_ffn_k_scale"][0], 2)], 1
                         ).astype(np.float32),
        "qsd1": np.stack([np.tile(ins["L0_ffn_q_scale"][1], 2),
                          np.tile(ins["L0_ffn_k_scale"][1], 2)], 1
                         ).astype(np.float32),
        "tabqd": tabq, "tabkd": tabk, "mgend": mg,
        "identd": np.eye(128).astype(ml_dtypes.bfloat16),
        "rotd": rot, "bo2d": bo2, "bc2d": bc2,
        "fw1_0": ins["L0_ffn_w1"][0], "fw1_1": ins["L0_ffn_w1"][1],
        "fw2_0": ins["L0_ffn_w2"][0], "fw2_1": ins["L0_ffn_w2"][1],
        "fb1a0": packr(ins["L0_ffn_b1"][0][:2730]),
        "fb1g0": packr(ins["L0_ffn_b1"][0][2730:]),
        "fb1a1": packr(ins["L0_ffn_b1"][1][:2730]),
        "fb1g1": packr(ins["L0_ffn_b1"][1][2730:]),
        "flg0": pack_vec(ins["L0_ffn_ln_g"][0], 8),
        "flb0": pack_vec(ins["L0_ffn_ln_b"][0], 8),
        "flg1": pack_vec(ins["L0_ffn_ln_g"][1], 8),
        "flb1": pack_vec(ins["L0_ffn_ln_b"][1], 8),
        "fb2_0": pack_vec(ins["L0_ffn_b2"][0], 8),
        "fb2_1": pack_vec(ins["L0_ffn_b2"][1], 8),
        "wkTd": np.ascontiguousarray(ins["L1_attn_Wk"].T),
        "wvTd": np.ascontiguousarray(ins["L1_attn_Wv"].T),
        "wrTd": np.ascontiguousarray(ins["L1_attn_Wr"].T),
        "woTd": np.ascontiguousarray(ins["L1_attn_Wo"].T),
        "tmd": pack_vec(tm, 8), "tdd": pack_vec(td_rep, 8),
        "tfd": pack_vec(tf_rep, 8),
        "lw1": ins["L1_ffn_w1"][0], "lw2": ins["L1_ffn_w2"][0],
        "lb1d": pack_vec(ins["L1_ffn_b1"][0], 32),
        "lb2d": pack_vec(ins["L1_ffn_b2"][0], 8),
        "lgmd": pack_vec(ins["L1_ffn_gamma"], 8),
        "lbtd": pack_vec(ins["L1_ffn_beta"], 8),
        "onesd": ones, "constsd": consts, "seld": sel_np,
    }
    maps = []
    for c in _CORES:
        b = c // 4
        mp = dict(common)
        mp["xin"] = np.ascontiguousarray(x0[b].T)
        maps.append(mp)
    res = run_bass_kernel_spmd(nc, maps, core_ids=_CORES)
    out = np.zeros((2, _T, 1024), np.float32)
    for c in _CORES:
        b, q = c // 4, c % 4
        out[b, q * 2048:(q + 1) * 2048] = res.results[c]["o_x4"][:, q * 2048:
                                                                (q + 1) * 2048].T
    return out


# version 20 (final)
# speedup vs baseline: 1.0786x; 1.0786x over previous
import numpy as np
import concourse.bass as bass
import concourse.mybir as mybir

F32 = mybir.dt.float32
BF16 = mybir.dt.bfloat16
AF = mybir.ActivationFunctionType
ALU = mybir.AluOpType

NCORES = 8


class Phase:
    def __init__(self, nc):
        self.nc = nc
        self.eng = {
            "sync": nc.sync,
            "tensor": nc.tensor,
            "vector": nc.vector,
            "scalar": nc.scalar,
            "gpsimd": nc.gpsimd,
        }
        self._ctxs = []
        self.bar_sem = self._sem("bar")
        self.nbar = 0

    def _sem(self, name):
        ctx = self.nc.semaphore(name)
        h = ctx.__enter__()
        self._ctxs.append(ctx)
        return h

    def barrier(self):
        self.nbar += 1
        target = 5 * self.nbar
        for e in self.eng.values():
            e.nop().then_inc(self.bar_sem, 1)
        for e in self.eng.values():
            e.wait_ge(self.bar_sem, target)

    def close(self):
        for ctx in reversed(self._ctxs):
            ctx.__exit__(None, None, None)


class Chain:

    def __init__(self, ph, name):
        self.sem = ph._sem(name)
        self.n = 0

    def inc(self, inst, k=1):
        inst.then_inc(self.sem, k)
        self.n += k
        return self.n

    def wait(self, eng, n=None):
        eng.wait_ge(self.sem, self.n if n is None else n)


class DQ:

    def __init__(self, ph, eng_name, name=None):
        self.e = ph.eng[eng_name]
        self.sem = ph._sem(name or ("dq_" + eng_name))
        self.n = 0

    def start(self, out, in_, serialize=True, **kw):
        if serialize and self.n:
            self.e.wait_ge(self.sem, self.n)
        self.e.dma_start(out=out, in_=in_, **kw).then_inc(self.sem, 16)
        self.n += 16
        return self.n

    def wait_all(self, eng):
        eng.wait_ge(self.sem, self.n)


def ceil_div(a, b):
    return (a + b - 1) // b


def ln_stats(ph, ch, slabs, tcols, ps_a, ps_b, sq_sb, ones_sb,
             mu_row, rstd_row, scr_row, eps_ap=None, nfeat=None,
             cast_bf=None, ones_bf=None):
    nkt = len(slabs)
    nfeat = nfeat if nfeat is not None else nkt * 128
    t, v, s = ph.eng["tensor"], ph.eng["vector"], ph.eng["scalar"]
    if cast_bf is not None:
        ch.wait(v)
        ev_c = {}; mm_c = {}
        for kt in range(nkt):
            sl = kt % 2
            cslot = cast_bf[:, sl * tcols:(sl + 1) * tcols]
            if kt >= 2:
                v.wait_ge(ch.sem, mm_c[kt - 2])
            ch.inc(v.tensor_copy(cslot, slabs[kt]))
            ev_c[kt] = ch.n
            t.wait_ge(ch.sem, ev_c[kt])
            ch.inc(t.matmul(ps_a[0:1, 0:tcols], ones_bf[:, 0:1], cslot,
                            start=(kt == 0), stop=(kt == nkt - 1)))
            mm_c[kt] = ch.n
        ev_q = {}; mm_q = {}
        for kt in range(nkt):
            sl = kt % 2
            qslot = cast_bf[:, (2 + sl) * tcols:(3 + sl) * tcols]
            if kt >= 2:
                s.wait_ge(ch.sem, mm_q[kt - 2])
            ch.inc(s.activation(out=qslot, in_=slabs[kt], func=AF.Square))
            ev_q[kt] = ch.n
            t.wait_ge(ch.sem, ev_q[kt])
            ch.inc(t.matmul(ps_b[0:1, 0:tcols], ones_bf[:, 0:1], qslot,
                            start=(kt == 0), stop=(kt == nkt - 1)))
            mm_q[kt] = ch.n
    else:
        for kt in range(nkt):
            i = t.matmul(ps_a[0:1, 0:tcols], ones_sb[:, 0:1], slabs[kt],
                         start=(kt == 0), stop=(kt == nkt - 1))
        ch.inc(i)
        for kt in range(nkt):
            ch.wait(s)
            ch.inc(s.activation(out=sq_sb[:, 0:tcols], in_=slabs[kt],
                                func=AF.Square))
            ch.wait(t)
            ch.inc(t.matmul(ps_b[0:1, 0:tcols], ones_sb[:, 0:1],
                            sq_sb[:, 0:tcols],
                            start=(kt == 0), stop=(kt == nkt - 1)))
            if kt < nkt - 1:
                ch.wait(s)
    ch.wait(v)
    inv_n = 1.0 / float(nfeat)
    mu = mu_row[0:1, 0:tcols]
    rs = rstd_row[0:1, 0:tcols]
    sc = scr_row[0:1, 0:tcols]
    ch.inc(v.tensor_scalar(mu, ps_a[0:1, 0:tcols], inv_n, None, ALU.mult))
    ch.wait(v)
    ch.inc(v.tensor_tensor(sc, mu, mu, ALU.mult))
    ch.wait(v)
    ch.inc(v.scalar_tensor_tensor(rs, ps_b[0:1, 0:tcols], inv_n, sc, ALU.mult,
                                  ALU.subtract))
    ch.wait(s)
    ch.inc(s.activation(out=sc, in_=rs, func=AF.Sqrt, bias=eps_ap))
    ch.wait(v)
    ch.inc(v.reciprocal(rs, sc))


def bcast_rows(ph, ch, rows_and_ps, tcols, ones_row):
    t = ph.eng["tensor"]
    ch.wait(t)
    for row, ps in rows_and_ps:
        i = t.matmul(ps[:, 0:tcols], ones_row[0:1, 0:128], row[0:1, 0:tcols],
                     start=True, stop=True)
    ch.inc(i)



def pack_vec(v, nblk):
    return np.ascontiguousarray(v.reshape(nblk, 128).T).astype(np.float32)




import contextlib
import numpy as np
import concourse.bass as bass
import concourse.mybir as mybir



def build_fused(nc, T=8192, TC=512):
    NTC = T // TC
    NKT = 8
    HB = 22
    HBW = [128] * 21 + [42]
    NW = T // 512
    ph = Phase(nc)
    dram = {}
    def din(name, shape, dt=F32):
        dram[name] = nc.dram_tensor(name, shape, dt, kind="ExternalInput").ap()
        return dram[name]
    def dscr(name, shape, dt=F32):
        dram[name] = nc.dram_tensor(name, shape, dt).ap()
        return dram[name]

    xin = din("xin", [1024, T])
    w_in = din("w_in", [1024, 4096]); w_xp = din("w_xp", [2048, 96])
    w_dt = din("w_dt", [64, 2048])
    cwd = din("cwd", [128, 64]); cbd = din("cbd", [128, 16])
    bdtd = din("bdtd", [128, 16]); alogd = din("alogd", [32768])
    ed = din("ed", [128, 2048]); rd = din("rd", [128, 2048], BF16)
    rbcd = din("rbcd", [16, 128]); dskd = din("dskd", [128, 16])
    woutd = din("woutd", [2048, 1024])
    wqkv_l = [din("wqkv0", [1024, 1536]), din("wqkv1", [1024, 1536])]
    wo_l = [din("wo0", [512, 1024]), din("wo1", [512, 1024])]
    qsd_l = [din("qsd0", [128, 2]), din("qsd1", [128, 2])]
    tabqd = din("tabqd", [128, 2 * T], BF16)
    tabkd = din("tabkd", [128, 2 * T], BF16)
    mgend = din("mgend", [128, 8 * 512], BF16)
    identd = din("identd", [128, 128], BF16)
    rotd = din("rotd", [128, 128]); bo2d = din("bo2d", [128, 2])
    bc2d = din("bc2d", [2, 128])
    fw1_l = [din("fw1_0", [1024, 5460]), din("fw1_1", [1024, 5460])]
    fw2_l = [din("fw2_0", [2730, 1024]), din("fw2_1", [2730, 1024])]
    fb1a_l = [din("fb1a0", [128, HB]), din("fb1a1", [128, HB])]
    fb1g_l = [din("fb1g0", [128, HB]), din("fb1g1", [128, HB])]
    flg_l = [din("flg0", [128, 8]), din("flg1", [128, 8])]
    flb_l = [din("flb0", [128, 8]), din("flb1", [128, 8])]
    fb2_l = [din("fb2_0", [128, 8]), din("fb2_1", [128, 8])]
    seld = din("seld", [128, 2048])
    wkTd = din("wkTd", [1024, 1024]); wvTd = din("wvTd", [1024, 1024])
    wrTd = din("wrTd", [1024, 1024]); woTd = din("woTd", [1024, 1024])
    tmd = din("tmd", [128, 8]); tdd = din("tdd", [128, 8]); tfd = din("tfd", [128, 8])
    lw1 = din("lw1", [1024, 4096]); lw2 = din("lw2", [4096, 1024])
    lb1d = din("lb1d", [128, 32]); lb2d = din("lb2d", [128, 8])
    lgmd = din("lgmd", [128, 8]); lbtd = din("lbtd", [128, 8])
    onesd = din("onesd", [128, 128]); constsd = din("constsd", [1, 8])
    o_x4 = nc.dram_tensor("o_x4", [1024, T], F32, kind="ExternalOutput").ap()

    s_d = dscr("s_d", [2048, T], BF16); s_u = dscr("s_u", [2048, T])
    s_du = dscr("s_du", [2048, T], BF16)
    s_z = dscr("s_z", [2048, T])
    s_B = dscr("s_B", [16, T], BF16); s_C = dscr("s_C", [16, T], BF16)
    s_m = dscr("s_m", [1024, T])
    s_x1 = dscr("s_x1", [1024, T]); s_xp = dscr("s_xp", [1024, T])
    s_xa = dscr("s_xa", [1024, T])
    s_qf = dscr("s_qf", [512, T], BF16); s_kf = dscr("s_kf", [512, T], BF16)
    s_vt = dscr("s_vt", [T, 512], BF16)
    s_x2 = dscr("s_x2", [1024, T]); s_x3 = dscr("s_x3", [1024, T])

    esg = contextlib.ExitStack()
    P = [esg.enter_context(nc.psum_tensor(f"ps{i}", [128, 512], F32))
         for i in range(8)]
    par_sb = esg.enter_context(nc.sbuf_tensor("par_g", [128, 430], F32))
    ones_row = esg.enter_context(nc.sbuf_tensor("ones_row", [1, 128], F32))
    rows = esg.enter_context(nc.sbuf_tensor("rows_g", [1, 2048], F32))
    rows2 = esg.enter_context(nc.sbuf_tensor("rows2_g", [2, 512], F32))
    scrA = esg.enter_context(nc.sbuf_tensor("scrA", [128, 512], F32))
    scrB = esg.enter_context(nc.sbuf_tensor("scrB", [128, 512], F32))
    scrC = esg.enter_context(nc.sbuf_tensor("scrC", [128, 512], F32))
    lnbf = esg.enter_context(nc.sbuf_tensor("lnbf_g", [128, 2048], BF16))
    onesb = esg.enter_context(nc.sbuf_tensor("onesb_g", [128, 1], BF16))
    ones_col = par_sb[:, 0:1]
    eps_ap = par_sb[0:1, 1:2]
    mu1, rs1 = rows[0:1, 0:512], rows[0:1, 512:1024]
    scrow = rows[0:1, 1024:1536]

    ld = DQ(ph, "sync", "ld")
    sdq = DQ(ph, "scalar", "sdq")
    tdq = DQ(ph, "tensor", "tdq")
    wld = [DQ(ph, "gpsimd", "wldA"), DQ(ph, "gpsimd", "wldB")]
    st = DQ(ph, "gpsimd", "st")
    st2 = DQ(ph, "sync", "st2")
    ch = Chain(ph, "ch")
    chS = Chain(ph, "chS")
    chV = Chain(ph, "chV")
    mm = Chain(ph, "mm")
    t, v, s, g, sy = (ph.eng["tensor"], ph.eng["vector"], ph.eng["scalar"],
                      ph.eng["gpsimd"], ph.eng["sync"])

    ld.start(ones_row[0:1, :], onesd[0:1, :], serialize=False)
    ld.start(ones_col, onesd[:, 0:1], serialize=False,
             allow_slow_non_contiguous=True)
    ld.start(eps_ap, constsd[0:1, 0:1], serialize=False,
             allow_slow_non_contiguous=True)
    wld[0].start(onesb[:, 0:1], onesd[:, 0:1], serialize=False,
                 allow_slow_non_contiguous=True)
    for e in ph.eng.values():
        ld.wait_all(e); wld[0].wait_all(e)

    kt3 = lambda buf, nkt: buf[:, :].rearrange("p (kt t) -> p kt t", kt=nkt)
    dr3 = lambda ap: ap.rearrange("(kt p) t -> p kt t", p=128)

    def stage_sync():
        for e in ph.eng.values():
            ld.wait_all(e); st.wait_all(e); st2.wait_all(e)
            wld[0].wait_all(e); wld[1].wait_all(e)
            sdq.wait_all(e); tdq.wait_all(e)
            e.wait_ge(ch.sem, ch.n)
            e.wait_ge(chS.sem, chS.n)
            e.wait_ge(chV.sem, chV.n)
            e.wait_ge(mm.sem, mm.n)
        ph.barrier()

    def ln_block(slabs, tcols, out_fn):
        ch.wait(t); ch.wait(s)
        ln_stats(ph, ch, slabs, tcols, P[6], P[7], scrA, ones_col,
                 mu1, rs1, scrow, eps_ap=eps_ap)
        bcast_rows(ph, ch, [(mu1, P[6]), (rs1, P[7])], tcols, ones_row)
        ch.wait(v)
        for kt in range(len(slabs)):
            out_fn(kt)

    with contextlib.ExitStack() as es:
        E = es.enter_context
        x_sb = E(nc.sbuf_tensor("m1_x", [128, NKT * TC], F32))
        lnx_bf = E(nc.sbuf_tensor("m1_lnx", [128, NKT * TC], BF16))
        win_sb = E(nc.sbuf_tensor("m1_win", [128, NKT * 4096], BF16))
        wxp_sb = E(nc.sbuf_tensor("m1_wxp", [128, 16 * 96], BF16))
        wdt_sb = E(nc.sbuf_tensor("m1_wdt", [64, 16 * 128], BF16))
        u_bf = E(nc.sbuf_tensor("m1_ubf", [128, 16 * TC], BF16))
        xin2 = E(nc.sbuf_tensor("m1_xin2", [128, 2 * (TC + 3)], F32))
        carry_sb = E(nc.sbuf_tensor("m1_car", [128, 48], F32))
        cscr = E(nc.sbuf_tensor("m1_cscr", [128, 2 * TC], F32))
        zstg = E(nc.sbuf_tensor("m1_zstg", [128, 2 * TC], F32))
        u_fs = E(nc.sbuf_tensor("m1_ufs", [128, 16 * TC], F32))
        dscr = E(nc.sbuf_tensor("m1_dscr", [128, 6 * TC], F32))
        xdbl_sb = E(nc.sbuf_tensor("m1_xdbl", [96, TC], F32))
        dtl_bf = E(nc.sbuf_tensor("m1_dtl", [64, TC], BF16))
        pp = E(nc.sbuf_tensor("m1_pp", [128, 100], F32))
        cw_sb, cb_sb, bdt_sb = pp[:, 0:64], pp[:, 64:80], pp[:, 80:96]

        wld[0].start(kt3(win_sb, NKT), dr3(w_in), serialize=False)
        wld[0].start(kt3(wxp_sb, 16), dr3(w_xp), serialize=False)
        wld[0].start(wdt_sb[:, :].rearrange("kk (kt o) -> kk kt o", kt=16),
                     w_dt.rearrange("kk (kt o) -> kk kt o", kt=16),
                     serialize=False)
        ld.start(cw_sb, cwd[:, :], serialize=False)
        ld.start(cb_sb, cbd[:, :], serialize=False)
        ld.start(bdt_sb, bdtd[:, :], serialize=False)
        for e in (t, v, s, g, sy):
            ld.wait_all(e); wld[0].wait_all(e)
        ch.wait(v)
        ch.inc(v.memset(carry_sb[:, :], 0.0))
        ph.barrier()

        for c in range(NTC):
            t0 = c * TC
            sy.wait_ge(ch.sem, ch.n); sy.wait_ge(mm.sem, mm.n)
            ld.start(kt3(x_sb, NKT), dr3(xin)[:, :, t0:t0 + TC])
            for e in (t, v, s):
                ld.wait_all(e); st.wait_all(e); st2.wait_all(e)
            xkt = lambda kt: x_sb[:, kt*TC:(kt+1)*TC]

            def apply_ln1(kt):
                ch.wait(v)
                ch.inc(v.tensor_tensor(scrA[:, 0:TC], xkt(kt), P[6][:, 0:TC],
                                       ALU.subtract))
                ch.wait(v)
                ch.inc(v.tensor_tensor(lnx_bf[:, kt*TC:(kt+1)*TC],
                                       scrA[:, 0:TC], P[7][:, 0:TC], ALU.mult))
            ln_block([xkt(kt) for kt in range(NKT)], TC, apply_ln1)
            lnx_ready = ch.n
            ph.barrier()

            ev_u = {}; conv_done = {}; silu_done = {}; ubf_done = {}
            for ob in range(16):
                par = ob % 2
                ps = P[0] if par == 0 else P[4]
                xv = xin2[:, par*(TC+3): par*(TC+3)+TC+3]
                sc = cscr[:, par*TC:(par+1)*TC]
                t.wait_ge(ch.sem, lnx_ready)
                if ob >= 2:
                    t.wait_ge(chS.sem, ev_u[ob-2])
                for kt in range(NKT):
                    i = t.matmul(ps[:, 0:TC],
                                 win_sb[:, kt*4096 + ob*128: kt*4096 + (ob+1)*128],
                                 lnx_bf[:, kt*TC:(kt+1)*TC],
                                 start=(kt == 0), stop=(kt == NKT-1))
                mm.inc(i)
                s.wait_ge(mm.sem, mm.n)
                if ob >= 2:
                    s.wait_ge(chV.sem, conv_done[ob-2])
                chS.inc(s.activation(out=xv[:, 3:3+TC], in_=ps[:, 0:TC],
                                     func=AF.Identity))
                ev_u[ob] = chS.n
                chV.inc(v.tensor_copy(xv[:, 0:3], carry_sb[:, ob*3:(ob+1)*3]))
                v.wait_ge(chS.sem, ev_u[ob])
                if ob >= 2:
                    v.wait_ge(chS.sem, silu_done[ob-2])
                chV.inc(v.tensor_scalar(sc[:, 0:TC], xv[:, 0:TC],
                                        cw_sb[:, 4*ob:4*ob+1], None, ALU.mult))
                for kk in range(1, 4):
                    chV.inc(v.scalar_tensor_tensor(sc[:, 0:TC],
                                                   xv[:, kk:kk+TC],
                                                   cw_sb[:, 4*ob+kk:4*ob+kk+1],
                                                   sc[:, 0:TC], ALU.mult, ALU.add))
                chV.inc(v.tensor_copy(carry_sb[:, ob*3:(ob+1)*3],
                                      xv[:, TC:TC+3]))
                conv_done[ob] = chV.n
                s.wait_ge(chV.sem, conv_done[ob])
                chS.inc(s.activation(out=u_fs[:, ob*TC:(ob+1)*TC],
                                     in_=sc[:, 0:TC],
                                     func=AF.Silu, bias=cb_sb[:, ob:ob+1]))
                silu_done[ob] = chS.n
                v.wait_ge(chS.sem, silu_done[ob])
                chV.inc(v.tensor_copy(u_bf[:, ob*TC:(ob+1)*TC],
                                      u_fs[:, ob*TC:(ob+1)*TC]))
                ubf_done[ob] = chV.n
                g.wait_ge(chS.sem, silu_done[ob])
                st.start(s_u[ob*128:(ob+1)*128, t0:t0+TC],
                         u_fs[:, ob*TC:(ob+1)*TC])
            ev_z = {}; st2n = {}
            for ob in range(16):
                par = ob % 2
                ps = P[1] if par == 0 else P[5]
                zs = zstg[:, par*TC:(par+1)*TC]
                if ob >= 2:
                    t.wait_ge(chS.sem, ev_z[ob-2])
                for kt in range(NKT):
                    i = t.matmul(ps[:, 0:TC],
                                 win_sb[:, kt*4096 + (16+ob)*128:
                                        kt*4096 + (17+ob)*128],
                                 lnx_bf[:, kt*TC:(kt+1)*TC],
                                 start=(kt == 0), stop=(kt == NKT-1))
                mm.inc(i)
                s.wait_ge(mm.sem, mm.n)
                if ob >= 2:
                    s.wait_ge(st2.sem, st2n[ob-2])
                chS.inc(s.activation(out=zs, in_=ps[:, 0:TC], func=AF.Identity))
                ev_z[ob] = chS.n
                sy.wait_ge(chS.sem, ev_z[ob])
                st2.start(s_z[ob*128:(ob+1)*128, t0:t0+TC], zs)
                st2n[ob] = st2.n
            t.wait_ge(chV.sem, ubf_done[15])
            for kt in range(16):
                i = t.matmul(P[2][0:96, 0:TC], wxp_sb[:, kt*96:(kt+1)*96],
                             u_bf[:, kt*TC:(kt+1)*TC],
                             start=(kt == 0), stop=(kt == 15))
            mm.inc(i)
            s.wait_ge(mm.sem, mm.n)
            chS.inc(s.activation(out=xdbl_sb[:, 0:TC], in_=P[2][0:96, 0:TC],
                                 func=AF.Identity))
            xdbl_ev = chS.n
            g.wait_ge(chS.sem, xdbl_ev)
            st.start(s_B[:, t0:t0+TC], xdbl_sb[64:80, 0:TC])
            st.start(s_C[:, t0:t0+TC], xdbl_sb[80:96, 0:TC])
            v.wait_ge(chS.sem, xdbl_ev)
            chV.inc(v.tensor_copy(dtl_bf[:, 0:TC], xdbl_sb[0:64, 0:TC]))
            dtl_ready = chV.n
            ev_d = {}; add_d = {}; stn_d = {}
            for ob in range(16):
                par = ob % 2
                ps = P[3] if par == 0 else P[2]
                sA = dscr[:, (par*3+0)*TC:(par*3+1)*TC]
                sB = dscr[:, (par*3+1)*TC:(par*3+2)*TC]
                sC = dscr[:, (par*3+2)*TC:(par*3+3)*TC]
                t.wait_ge(chV.sem, dtl_ready)
                if ob >= 2:
                    t.wait_ge(chS.sem, ev_d[ob-2])
                mm.inc(t.matmul(ps[:, 0:TC], wdt_sb[:, ob*128:(ob+1)*128],
                                dtl_bf[:, 0:TC], start=True, stop=True))
                s.wait_ge(mm.sem, mm.n)
                if ob >= 2:
                    s.wait_ge(chV.sem, add_d[ob-2])
                    s.wait_ge(st.sem, stn_d[ob-2])
                chS.inc(s.activation(out=sA, in_=ps[:, 0:TC],
                                     func=AF.Abs, bias=bdt_sb[:, ob:ob+1]))
                chS.inc(s.activation(out=sA, in_=sA, func=AF.Exp, scale=-1.0))
                chS.inc(s.activation(out=sA, in_=sA, func=AF.Ln,
                                     bias=ones_col[:, 0:1]))
                chS.inc(s.activation(out=sB, in_=ps[:, 0:TC],
                                     func=AF.Relu, bias=bdt_sb[:, ob:ob+1]))
                ev_d[ob] = chS.n
                v.wait_ge(chS.sem, ev_d[ob])
                if ob >= 2:
                    v.wait_ge(st.sem, stn_d[ob-2])
                chV.inc(v.tensor_tensor(sB, sA, sB, ALU.add))
                chV.inc(v.tensor_tensor(sC, sB, u_fs[:, ob*TC:(ob+1)*TC],
                                        ALU.mult))
                add_d[ob] = chV.n
                g.wait_ge(chV.sem, add_d[ob])
                st.start(s_d[ob*128:(ob+1)*128, t0:t0+TC], sB)
                st.start(s_du[ob*128:(ob+1)*128, t0:t0+TC], sC)
                stn_d[ob] = st.n
            ph.barrier()
        stage_sync()

    TC2 = 1024
    NTC2 = T // TC2
    with contextlib.ExitStack() as es:
        E = es.enter_context
        z1_sb = E(nc.sbuf_tensor("m2_z1", [128, TC2], F32))
        u1_sb = E(nc.sbuf_tensor("m2_u1", [128, TC2], F32))
        bhat_sb = E(nc.sbuf_tensor("m2_bh", [128, TC2], BF16))
        chat_sb = E(nc.sbuf_tensor("m2_ch", [128, TC2], BF16))
        dA_sb = E(nc.sbuf_tensor("m2_dA", [128, 8 * TC2], BF16))
        dBu_sb = E(nc.sbuf_tensor("m2_dBu", [128, 8 * TC2], BF16))
        h_sb = E(nc.sbuf_tensor("m2_h", [128, 2 * TC2], BF16))
        hc_sb = E(nc.sbuf_tensor("m2_hc", [128, 2 * TC2], BF16))
        y2_bf = E(nc.sbuf_tensor("m2_y2", [128, 16 * TC2], BF16))
        szz_sb = E(nc.sbuf_tensor("m2_szz", [128, TC2], F32))
        t1_sb = E(nc.sbuf_tensor("m2_t1", [128, TC2], F32))
        stg = E(nc.sbuf_tensor("m2_stg", [128, 2 * TC2], F32))
        hcar_sb = E(nc.sbuf_tensor("m2_hcar", [128, 256], F32))
        r_sb = E(nc.sbuf_tensor("m2_r", [128, 2048], BF16))
        sel_sb = E(nc.sbuf_tensor("m2_sel", [128, 2048], BF16))
        dtb_sb = E(nc.sbuf_tensor("m2_dtb", [128, 2 * TC2], BF16))
        dub_sb = E(nc.sbuf_tensor("m2_dub", [128, 2 * TC2], BF16))
        av_sb = E(nc.sbuf_tensor("m2_av", [128, 272], F32))
        wo_sb = E(nc.sbuf_tensor("m2_wo", [128, 16 * 1024], BF16))
        ds_sb = av_sb[:, 256:272]

        ld.start(r_sb[:, :], rd[:, :], serialize=False)
        wld[0].start(sel_sb[:, :], seld[:, :], serialize=False)
        ld.start(av_sb[:, 0:256], alogd.rearrange("(st p) -> p st", p=128),
                 serialize=False, allow_slow_non_contiguous=True)
        ld.start(ds_sb, dskd[:, :], serialize=False)
        wld[0].start(kt3(wo_sb, 16), dr3(woutd), serialize=False)
        for e in (t, v, s, g, sy):
            ld.wait_all(e); wld[0].wait_all(e)
        ch.wait(s)
        ch.inc(s.activation(out=av_sb[:, 0:256], in_=av_sb[:, 0:256], func=AF.Exp))
        ch.wait(v)
        ch.inc(v.tensor_scalar(av_sb[:, 0:256], av_sb[:, 0:256], -1.0, None,
                               ALU.mult))
        ch.wait(v)
        ch.inc(v.memset(hcar_sb[:, :], 0.0))
        ph.barrier()

        for c in range(NTC2):
            t0 = c * TC2
            sy.wait_ge(ch.sem, ch.n)
            sy.wait_ge(chS.sem, chS.n)
            sy.wait_ge(chV.sem, chV.n)
            sy.wait_ge(mm.sem, mm.n)
            ld.start(bhat_sb[:, :],
                     s_B[0:16, t0:t0+TC2].unsqueeze(0).broadcast_to([8, 16, TC2]))
            sdq.start(chat_sb[:, :],
                      s_C[0:16, t0:t0+TC2].unsqueeze(0).broadcast_to([8, 16, TC2]))
            for e in (t, v, s):
                ld.wait_all(e); sdq.wait_all(e); st.wait_all(e)

            def epilogue(dt_i):
                py = (P[0], P[1])
                sy.wait_ge(chS.sem, chS.n)
                ld.start(z1_sb[:, :], s_z[dt_i*128:(dt_i+1)*128, t0:t0+TC2])
                sy.wait_ge(chV.sem, chV.n)
                ld.start(u1_sb[:, :], s_u[dt_i*128:(dt_i+1)*128, t0:t0+TC2])
                ld.wait_all(s)
                chS.inc(s.activation(out=szz_sb[:, 0:TC2], in_=z1_sb[:, 0:TC2],
                                     func=AF.Silu))
                v.wait_ge(mm.sem, mm.n)
                v.wait_ge(chS.sem, chS.n)
                ld.wait_all(v)
                for hf in range(2):
                    v.scalar_tensor_tensor(t1_sb[:, hf*512:(hf+1)*512],
                                           u1_sb[:, hf*512:(hf+1)*512],
                                           ds_sb[:, dt_i:dt_i+1],
                                           py[hf][:, 0:512], ALU.mult, ALU.add)
                chV.inc(v.tensor_tensor(y2_bf[:, dt_i*TC2:(dt_i+1)*TC2],
                                        t1_sb[:, 0:TC2], szz_sb[:, 0:TC2],
                                        ALU.mult))

            exp_done = {}; v_done = {}; y_done = {}
            selB_done = {}; scan_evs = {}
            NSLOT = 8
            dt_ldn = {}
            for pre in range(2):
                ld.start(dub_sb[:, pre*TC2:(pre+1)*TC2],
                         s_du[pre*128:(pre+1)*128, t0:t0+TC2])
                dt_ldn[pre] = ld.n
            QS = [ld, sdq, st2]
            need_a = {}
            issued = -1
            def issue_dA(si):
                slot = si % NSLOT
                qa = QS[si % 3]
                if si >= NSLOT:
                    qa.e.wait_ge(chV.sem, scan_evs[si - NSLOT])
                src_a = s_d[si*8:si*8 + 8, t0:t0+TC2].unsqueeze(1
                            ).broadcast_to([8, 16, TC2])
                qa.start(dA_sb[:, slot*TC2:(slot+1)*TC2], src_a)
                need_a[si] = (qa, qa.n)
            def emit_du(si):
                dt_j = si // 16
                st_j = si % 16
                psBj = (P[4], P[5]) if si % 2 == 0 else (P[2], P[3])
                selj = sel_sb[:, st_j*128:(st_j+1)*128]
                dubj = dub_sb[:, (dt_j % 2)*TC2:(dt_j % 2 + 1)*TC2]
                if st_j == 0:
                    t.wait_ge(ld.sem, dt_ldn[dt_j])
                if si >= 2:
                    t.wait_ge(chV.sem, scan_evs[si-2])
                for hf in range(2):
                    mm.inc(t.matmul(psBj[hf][:, 0:512], selj,
                                    dubj[:, hf*512:(hf+1)*512],
                                    start=True, stop=True))
                selB_done[si] = mm.n
            emit_du(0)
            for sti in range(256):
                dt_i = sti // 16
                st16 = sti % 16
                slot = sti % NSLOT
                seg_dA = dA_sb[:, slot*TC2:(slot+1)*TC2]
                seg_du = dBu_sb[:, slot*TC2:(slot+1)*TC2]
                par = sti % 2
                segh = lambda buf: buf[:, par*TC2:(par+1)*TC2]
                while issued < min(255, sti + NSLOT - 1):
                    issued += 1
                    issue_dA(issued)
                if st16 == 0 and 1 <= dt_i < 15:
                    nxt = dt_i + 1
                    sy.wait_ge(mm.sem, selB_done[(dt_i-1)*16 + 15])
                    ld.start(dub_sb[:, (nxt % 2)*TC2:(nxt % 2 + 1)*TC2],
                             s_du[nxt*128:(nxt+1)*128, t0:t0+TC2])
                    dt_ldn[nxt] = ld.n
                qa, na = need_a[sti]
                s.wait_ge(qa.sem, na)
                chS.inc(s.activation(out=seg_dA, in_=seg_dA,
                                     func=AF.Exp, scale=av_sb[:, sti:sti+1]))
                exp_done[sti] = chS.n
                v.wait_ge(mm.sem, selB_done[sti])
                v.wait_ge(chS.sem, exp_done[sti])
                if sti >= 2:
                    v.wait_ge(chV.sem, v_done[sti-2])
                psBc = (P[4], P[5]) if par == 0 else (P[2], P[3])
                for hf in range(2):
                    v.scalar_tensor_tensor(seg_du[:, hf*512:(hf+1)*512],
                                           psBc[hf][:, 0:512], 1.0,
                                           bhat_sb[:, hf*512:(hf+1)*512],
                                           ALU.mult, ALU.mult)
                chV.inc(v.tensor_tensor_scan(segh(h_sb), seg_dA, seg_du,
                                             hcar_sb[:, sti:sti+1], ALU.mult,
                                             ALU.add))
                scan_evs[sti] = chV.n
                v.tensor_copy(hcar_sb[:, sti:sti+1], segh(h_sb)[:, TC2-1:TC2])
                g.wait_ge(chV.sem, scan_evs[sti])
                if sti >= 2:
                    g.wait_ge(mm.sem, y_done[sti-2])
                chV.inc(g.tensor_tensor(segh(hc_sb), segh(h_sb),
                                        chat_sb[:, 0:TC2], ALU.mult))
                v_done[sti] = chV.n
                if sti + 1 < 256:
                    emit_du(sti + 1)
                t.wait_ge(chV.sem, v_done[sti])
                py = (P[0], P[1])
                for hf in range(2):
                    i = t.matmul(py[hf][:, 0:512],
                                 r_sb[:, st16*128:(st16+1)*128],
                                 hc_sb[:, par*TC2 + hf*512: par*TC2 + (hf+1)*512],
                                 start=(st16 == 0), stop=(st16 == 15))
                mm.inc(i)
                y_done[sti] = mm.n
                if st16 == 15:
                    epilogue(dt_i)
            y2_ready = chV.n
            add_done = {}
            for ob in range(8):
                for hf in range(2):
                    idx = ob * 2 + hf
                    psw = P[4] if idx % 2 == 0 else P[5]
                    t.wait_ge(chV.sem, y2_ready)
                    if idx >= 2:
                        t.wait_ge(chS.sem, add_done[idx-2])
                    for dt_i in range(16):
                        i = t.matmul(psw[:, 0:512],
                                     wo_sb[:, dt_i*1024 + ob*128:
                                           dt_i*1024 + (ob+1)*128],
                                     y2_bf[:, dt_i*TC2 + hf*512:
                                           dt_i*TC2 + (hf+1)*512],
                                     start=(dt_i == 0), stop=(dt_i == 15))
                    mm.inc(i)
                    s.wait_ge(mm.sem, mm.n)
                    s.wait_ge(st.sem, max(0, st.n - 16))
                    chS.inc(s.activation(out=stg[:, (idx % 2)*TC2//1:
                                             (idx % 2)*TC2//1 + 512]
                                         if False else
                                         stg[:, (idx % 2)*1024:(idx % 2)*1024+512],
                                         in_=psw[:, 0:512], func=AF.Identity))
                    add_done[idx] = chS.n
                    g.wait_ge(chS.sem, chS.n)
                    st.start(s_m[ob*128:(ob+1)*128, t0 + hf*512: t0 + (hf+1)*512],
                             stg[:, (idx % 2)*1024:(idx % 2)*1024+512])
            ph.barrier()
        stage_sync()

    with contextlib.ExitStack() as es:
        E = es.enter_context
        mslab = E(nc.sbuf_tensor("s3_m", [128, NKT * TC], F32))
        xslab = E(nc.sbuf_tensor("s3_x", [128, NKT * TC], F32))
        for c in range(NTC):
            t0 = c * TC
            sy.wait_ge(ch.sem, ch.n)
            st.wait_all(sy); st2.wait_all(sy)
            ld.start(kt3(mslab, NKT), dr3(s_m)[:, :, t0:t0+TC])
            ld.start(kt3(xslab, NKT), dr3(xin)[:, :, t0:t0+TC])
            for e in (t, v, s):
                ld.wait_all(e); st.wait_all(e); st2.wait_all(e)
            mkt = lambda kt: mslab[:, kt*TC:(kt+1)*TC]
            xkt = lambda kt: xslab[:, kt*TC:(kt+1)*TC]

            def apply1(kt):
                ch.wait(v)
                ch.inc(v.tensor_tensor(scrA[:, 0:TC], mkt(kt), P[6][:, 0:TC],
                                       ALU.subtract))
                ch.wait(v)
                ch.inc(v.tensor_tensor(scrB[:, 0:TC], scrA[:, 0:TC],
                                       P[7][:, 0:TC], ALU.mult))
                ch.wait(v)
                ch.inc(v.tensor_tensor(mkt(kt), scrB[:, 0:TC], xkt(kt), ALU.add))
            ln_block([mkt(kt) for kt in range(NKT)], TC, apply1)
            g.wait_ge(ch.sem, ch.n)
            st.start(dr3(s_x1)[:, :, t0:t0+TC], kt3(mslab, NKT))

            def apply2(kt):
                ch.wait(v)
                ch.inc(v.tensor_tensor(scrA[:, 0:TC], mkt(kt), P[6][:, 0:TC],
                                       ALU.subtract))
                ch.wait(v)
                ch.inc(v.tensor_tensor(xslab[:, kt*TC:(kt+1)*TC], scrA[:, 0:TC],
                                       P[7][:, 0:TC], ALU.mult))
            ch.wait(t); ch.wait(s)
            ln_stats(ph, ch, [mkt(kt) for kt in range(NKT)], TC, P[6], P[7],
                     scrA, ones_col, mu1, rs1, scrow, eps_ap=eps_ap)
            bcast_rows(ph, ch, [(mu1, P[6]), (rs1, P[7])], TC, ones_row)
            ch.wait(v)
            for kt in range(NKT):
                apply2(kt)
            sy.wait_ge(ch.sem, ch.n)
            st2.start(dr3(s_xp)[:, :, t0:t0+TC], kt3(xslab, NKT))
            ph.barrier()
        stage_sync()

    def attn_stage(layer, src_xp, dst_xa):
        wqkv, wo, qsd = wqkv_l[layer], wo_l[layer], qsd_l[layer]
        with contextlib.ExitStack() as es:
            E = es.enter_context
            xp_sb = E(nc.sbuf_tensor(f"a{layer}_xp", [128, NKT * (TC + 1)], F32))
            xp_bf = E(nc.sbuf_tensor(f"a{layer}_xpb", [128, NKT * (TC + 1)], BF16))
            wq_sb = E(nc.sbuf_tensor(f"a{layer}_wq", [128, NKT * 1536], BF16))
            wo_bf = E(nc.sbuf_tensor(f"a{layer}_wo", [128, 4 * 1024], BF16))
            tabq_sb = E(nc.sbuf_tensor(f"a{layer}_tq", [128, 2 * T], BF16))
            tabk_sb = E(nc.sbuf_tensor(f"a{layer}_tk", [128, 2 * T], BF16))
            mg_sb = E(nc.sbuf_tensor(f"a{layer}_mg", [128, 8 * 512], BF16))
            id_sb = E(nc.sbuf_tensor(f"a{layer}_id", [128, 128], BF16))
            rot_sb = E(nc.sbuf_tensor(f"a{layer}_rot", [128, 128], F32))
            bo2_sb = E(nc.sbuf_tensor(f"a{layer}_bo2", [128, 2], F32))
            bc2_sb = E(nc.sbuf_tensor(f"a{layer}_bc2", [2, 128], F32))
            qf_sb = E(nc.sbuf_tensor(f"a{layer}_qf", [128, 4 * 512], BF16))
            kf_sb = E(nc.sbuf_tensor(f"a{layer}_kf", [128, 4 * 1024], BF16))
            vtok_sb = E(nc.sbuf_tensor(f"a{layer}_vt", [128, 8 * 512], BF16))
            P_bf = E(nc.sbuf_tensor(f"a{layer}_P", [128, 2 * 512], BF16))
            PT_bf = E(nc.sbuf_tensor(f"a{layer}_PT", [128, 8 * 128], BF16))
            otok_bf = E(nc.sbuf_tensor(f"a{layer}_ot", [128, 512], BF16))
            ofw_bf = E(nc.sbuf_tensor(f"a{layer}_of", [128, 4 * 512], BF16))
            rinv_sb = E(nc.sbuf_tensor(f"a{layer}_ri", [128, 16], F32))
            qs_sb = E(nc.sbuf_tensor(f"a{layer}_qs", [128, 2], F32))
            xres = E(nc.sbuf_tensor(f"a{layer}_xr", [128, NKT * 512], F32))

            wld[0].start(kt3(wo_bf, 4), dr3(wo), serialize=False)
            wld[1].start(kt3(wq_sb, NKT), dr3(wqkv), serialize=False)
            ld.start(tabq_sb[:, :], tabqd[:, :], serialize=False)
            ld.start(tabk_sb[:, :], tabkd[:, :], serialize=False)
            ld.start(mg_sb[:, :], mgend[:, :], serialize=False)
            ld.start(id_sb[:, :], identd[:, :], serialize=False)
            ld.start(rot_sb[:, :], rotd[:, :], serialize=False)
            ld.start(bo2_sb[:, :], bo2d[:, :], serialize=False)
            ld.start(bc2_sb[:, :], bc2d[:, :], serialize=False)
            ld.start(qs_sb[:, :], qsd[:, :], serialize=False)
            for e in ph.eng.values():
                ld.wait_all(e); wld[0].wait_all(e)
            ph.barrier()

            def qknorm(c5, is_q, psrc):
                s.wait_ge(mm.sem, mm.n)
                ch.wait(s)
                ch.inc(s.activation(out=scrA[:, 0:512], in_=psrc[:, 0:512],
                                    func=AF.Identity))
                ch.wait(s)
                ch.inc(s.activation(out=scrB[:, 0:512], in_=scrA[:, 0:512],
                                    func=AF.Square))
                ch.wait(t)
                mm.inc(t.matmul(P[6][0:2, 0:512], bo2_sb[:, 0:2],
                                scrB[:, 0:512], start=True, stop=True))
                s.wait_ge(mm.sem, mm.n)
                ch.wait(s)
                ch.inc(s.activation(out=rows2[0:2, 0:512], in_=P[6][0:2, 0:512],
                                    func=AF.Sqrt))
                ch.wait(v)
                ch.inc(v.tensor_scalar(rows2[0:2, 0:512], rows2[0:2, 0:512],
                                       1e-12, None, ALU.max))
                ch.wait(v)
                ch.inc(v.reciprocal(rows2[0:2, 0:512], rows2[0:2, 0:512]))
                ch.wait(t)
                mm.inc(t.matmul(P[7][:, 0:512], bc2_sb[:, :],
                                rows2[0:2, 0:512], start=True, stop=True))
                v.wait_ge(mm.sem, mm.n)
                ch.wait(v)
                ch.inc(v.scalar_tensor_tensor(
                    scrB[:, 0:512], scrA[:, 0:512],
                    qs_sb[:, 0:1] if is_q else qs_sb[:, 1:2],
                    P[7][:, 0:512], ALU.mult, ALU.mult))
                ch.wait(t)
                mm.inc(t.matmul(P[6][:, 0:512], rot_sb[:, :], scrB[:, 0:512],
                                start=True, stop=True))
                tab = tabq_sb if is_q else tabk_sb
                tcos = tab[:, c5*512:(c5+1)*512]
                tsin = tab[:, T + c5*512: T + (c5+1)*512]
                ch.wait(v)
                ch.inc(v.tensor_tensor(scrC[:, 0:512], scrB[:, 0:512], tcos,
                                       ALU.mult))
                v.wait_ge(mm.sem, mm.n)
                ch.wait(v)
                ch.inc(v.scalar_tensor_tensor(scrB[:, 0:512], P[6][:, 0:512],
                                              1.0, tsin, ALU.mult, ALU.mult))
                ch.wait(v)

            for c5 in range(NTC):
                xc0 = c5 * TC
                sy.wait_ge(ch.sem, ch.n); sy.wait_ge(mm.sem, mm.n)
                ld.start(kt3(xp_sb, NKT)[:, :, 1:TC+1],
                         dr3(src_xp)[:, :, xc0:xc0+TC])
                if c5 == 0:
                    ch.wait(v)
                    ch.inc(v.memset(kt3(xp_sb, NKT)[:, :, 0:1], 0.0))
                else:
                    ld.start(kt3(xp_sb, NKT)[:, :, 0:1],
                             dr3(src_xp)[:, :, xc0-1:xc0],
                             allow_slow_non_contiguous=True)
                for e in (t, v, s):
                    ld.wait_all(e); st.wait_all(e); st2.wait_all(e)
                for kt in range(NKT):
                    ch.wait(v)
                    ch.inc(v.tensor_copy(xp_bf[:, kt*(TC+1):(kt+1)*(TC+1)],
                                         xp_sb[:, kt*(TC+1):(kt+1)*(TC+1)]))
                xpb_ready = ch.n
                for ob in range(12):
                    t.wait_ge(ch.sem, ch.n)
                    t.wait_ge(chS.sem, chS.n)
                    t.wait_ge(chV.sem, chV.n)
                    for kt in range(NKT):
                        off = 1 if kt < 4 else 0
                        i = t.matmul(P[0][:, 0:TC],
                                     wq_sb[:, kt*1536 + ob*128:
                                           kt*1536 + (ob+1)*128],
                                     xp_bf[:, kt*(TC+1) + off: kt*(TC+1) + off + TC],
                                     start=(kt == 0), stop=(kt == NKT-1))
                    mm.inc(i)
                    if ob < 4:
                        qknorm(c5, True, P[0])
                        ch.inc(v.tensor_tensor(
                            qf_sb[:, ob*512:(ob+1)*512], scrC[:, 0:512],
                            scrB[:, 0:512], ALU.add))
                        g.wait_ge(ch.sem, ch.n)
                        st.start(s_qf[ob*128:(ob+1)*128, xc0:xc0+TC],
                                 qf_sb[:, ob*512:(ob+1)*512])
                    elif ob < 8:
                        qknorm(c5, False, P[0])
                        ch.inc(v.tensor_tensor(
                            kf_sb[:, (ob-4)*1024:(ob-4)*1024+512],
                            scrC[:, 0:512], scrB[:, 0:512], ALU.add))
                        g.wait_ge(ch.sem, ch.n)
                        st.start(s_kf[(ob-4)*128:(ob-3)*128, xc0:xc0+TC],
                                 kf_sb[:, (ob-4)*1024:(ob-4)*1024+512])
                    else:
                        s.wait_ge(mm.sem, mm.n)
                        ch.wait(s)
                        ch.inc(s.activation(out=P_bf[:, 0:512],
                                            in_=P[0][:, 0:TC], func=AF.Identity))
                        ch.wait(t)
                        for js in range(4):
                            t.wait_ge(ch.sem, ch.n)
                            i = t.transpose(P[1].bitcast(BF16)[:, 0:128],
                                            P_bf[:, js*128:(js+1)*128],
                                            id_sb[:, :])
                            mm.inc(i)
                            s.wait_ge(mm.sem, mm.n)
                            ch.wait(s)
                            ch.inc(s.activation(
                                out=vtok_sb[:, js*512 + (ob-8)*128:
                                            js*512 + (ob-7)*128],
                                in_=P[1].bitcast(BF16)[:, 0:128],
                                func=AF.Identity))
                        if ob == 11:
                            g.wait_ge(ch.sem, ch.n)
                            st.start(s_vt[xc0: xc0 + 512, :].rearrange(
                                         "(js p) e -> p js e", p=128),
                                     vtok_sb[:, 0:4*512].rearrange(
                                         "p (js e) -> p js e", js=4))
                ph.barrier()

            for wq in range(NW):
                sy.wait_ge(ch.sem, ch.n); sy.wait_ge(mm.sem, mm.n)
                st.wait_all(sy); st2.wait_all(sy)
                ld.start(kt3(qf_sb, 4), s_qf.rearrange(
                             "(kt p) t -> p kt t", p=128)[:, :, wq*512:(wq+1)*512])
                klo = max(0, (wq - 1) * 512)
                ld.start(kt3(kf_sb, 4)[:, :, 0:(wq+1)*512 - klo],
                         s_kf.rearrange("(kt p) t -> p kt t", p=128)[
                             :, :, klo:(wq+1)*512])
                njt = ((wq + 1) * 512 - klo) // 128
                ld.start(vtok_sb[:, :].rearrange("p (jt e) -> p jt e", jt=8)[
                             :, 0:njt, :],
                         s_vt[klo:(wq+1)*512, :].rearrange(
                             "(jt p) e -> p jt e", p=128))
                for e in (t, v, s):
                    ld.wait_all(e); st.wait_all(e); st2.wait_all(e)
                nj = ((wq + 1) * 512 - klo) // 128
                nchunks = 1 if wq == 0 else 2
                for bi in range(4):
                    for h in range(8):
                        hb = 64 * (h % 2)
                        ht = h // 2
                        t.wait_ge(ch.sem, ch.n)
                        lq = qf_sb[hb:hb+64, ht*512 + bi*128: ht*512 + (bi+1)*128]
                        for cc in range(nchunks):
                            mm.inc(t.matmul(
                                P[cc][:, 0:512], lq,
                                kf_sb[hb:hb+64, ht*1024 + cc*512: ht*1024 + (cc+1)*512],
                                start=True, stop=True))
                        v.wait_ge(mm.sem, mm.n)
                        for cc in range(nchunks):
                            mi = (bi*2 + (cc + (2 - nchunks)))
                            ch.wait(v)
                            ch.inc(v.tensor_tensor(
                                P[cc][:, 0:512], P[cc][:, 0:512],
                                mg_sb[:, mi*512:(mi+1)*512], ALU.add))
                        for cc in range(nchunks):
                            ch.wait(v)
                            ch.inc(v.reduce_max(scrA[:, cc:cc+1], P[cc][:, 0:512],
                                                axis=mybir.AxisListType.X))
                        if nchunks == 2:
                            ch.wait(v)
                            ch.inc(v.tensor_tensor(scrA[:, 2:3], scrA[:, 0:1],
                                                   scrA[:, 1:2], ALU.max))
                            mx = scrA[:, 2:3]
                        else:
                            mx = scrA[:, 0:1]
                        ch.wait(v)
                        ch.inc(v.tensor_scalar(scrA[:, 3:4], mx, -1.0, None,
                                               ALU.mult))
                        s.wait_ge(mm.sem, mm.n)
                        for cc in range(nchunks):
                            ch.wait(s)
                            ch.inc(s.activation(out=P_bf[:, cc*512:(cc+1)*512],
                                                in_=P[cc][:, 0:512], func=AF.Exp,
                                                bias=scrA[:, 3:4],
                                                accum_out=scrA[:, 4+cc:5+cc]))
                        if nchunks == 2:
                            ch.wait(v)
                            ch.inc(v.tensor_tensor(scrA[:, 6:7], scrA[:, 4:5],
                                                   scrA[:, 5:6], ALU.add))
                            sm = scrA[:, 6:7]
                        else:
                            sm = scrA[:, 4:5]
                        ch.wait(v)
                        ch.inc(v.reciprocal(rinv_sb[:, h:h+1], sm))
                        ch.wait(t)
                        pt_ev = {}
                        for js in range(nj):
                            pspt = P[2] if js % 2 == 0 else P[7]
                            if js >= 2:
                                t.wait_ge(chS.sem, pt_ev[js-2])
                            i = t.transpose(pspt.bitcast(BF16)[:, 0:128],
                                            P_bf[:, js*128:(js+1)*128],
                                            id_sb[:, :])
                            mm.inc(i)
                            s.wait_ge(mm.sem, mm.n)
                            chS.inc(s.activation(out=PT_bf[:, js*128:(js+1)*128],
                                                 in_=pspt.bitcast(BF16)[:, 0:128],
                                                 func=AF.Identity))
                            pt_ev[js] = chS.n
                        t.wait_ge(chS.sem, chS.n)
                        for js in range(nj):
                            i = t.matmul(P[4][:, 64*h:64*h+64],
                                         PT_bf[:, js*128:(js+1)*128],
                                         vtok_sb[:, js*512 + 64*h: js*512 + 64*h + 64],
                                         start=(js == 0), stop=(js == nj-1))
                        mm.inc(i)
                    v.wait_ge(mm.sem, mm.n)
                    for h in range(8):
                        ch.wait(v)
                        ch.inc(v.tensor_scalar(otok_bf[:, 64*h:64*h+64],
                                               P[4][:, 64*h:64*h+64],
                                               rinv_sb[:, h:h+1], None, ALU.mult))
                    ch.wait(t)
                    for et in range(4):
                        t.wait_ge(ch.sem, ch.n)
                        i = t.transpose(P[3].bitcast(BF16)[:, 0:128],
                                        otok_bf[:, et*128:(et+1)*128], id_sb[:, :])
                        mm.inc(i)
                        s.wait_ge(mm.sem, mm.n)
                        ch.wait(s)
                        ch.inc(s.activation(out=ofw_bf[:, et*512 + bi*128:
                                                       et*512 + (bi+1)*128],
                                            in_=P[3].bitcast(BF16)[:, 0:128],
                                            func=AF.Identity))
                sy.wait_ge(ch.sem, ch.n)
                st.wait_all(sy)
                ld.start(kt3(xres, NKT), dr3(src_xp)[:, :, wq*512:(wq+1)*512])
                for e in (t, v, s):
                    ld.wait_all(e)
                t.wait_ge(ch.sem, ch.n)
                add_done = {}
                for ob in range(NKT):
                    par = ob % 2
                    psw = P[5] if par == 0 else P[6]
                    if ob >= 2:
                        t.wait_ge(ch.sem, add_done[ob-2])
                    for et in range(4):
                        i = t.matmul(psw[:, 0:512],
                                     wo_bf[:, et*1024 + ob*128: et*1024 + (ob+1)*128],
                                     ofw_bf[:, et*512:(et+1)*512],
                                     start=(et == 0), stop=(et == 3))
                    mm.inc(i)
                    v.wait_ge(mm.sem, mm.n)
                    v.wait_ge(st.sem, max(0, st.n - 16))
                    ch.wait(v)
                    ch.inc(v.tensor_tensor(xres[:, ob*512:(ob+1)*512],
                                           psw[:, 0:512],
                                           xres[:, ob*512:(ob+1)*512], ALU.add))
                    add_done[ob] = ch.n
                    g.wait_ge(ch.sem, ch.n)
                    st.start(dr3(dst_xa)[:, ob, wq*512:(wq+1)*512],
                             xres[:, ob*512:(ob+1)*512])
                ph.barrier()
            stage_sync()

    def geglu_stage(layer, src_xa, dst, last):
        w1, w2 = fw1_l[layer], fw2_l[layer]
        with contextlib.ExitStack() as es:
            E = es.enter_context
            xa_sb = E(nc.sbuf_tensor(f"g{layer}_xa", [128, NKT * (TC + 1)], F32))
            x1_sb = E(nc.sbuf_tensor(f"g{layer}_x1", [128, NKT * TC], F32))
            tgt_bf = E(nc.sbuf_tensor(f"g{layer}_tg", [128, NKT * TC], BF16))
            h_bf = E(nc.sbuf_tensor(f"g{layer}_h", [128, HB * TC], BF16))
            out_sb = E(nc.sbuf_tensor(f"g{layer}_o", [128, NKT * TC], F32))
            w1_sb = E(nc.sbuf_tensor(f"g{layer}_w1", [128, NKT * 5460], BF16))
            w2buf = E(nc.sbuf_tensor(f"g{layer}_w2", [128, 2 * HB * 128], BF16))
            pp = E(nc.sbuf_tensor(f"g{layer}_pp", [128, 2 * HB + 24], F32))
            b1a_sb = pp[:, 0:HB]
            b1g_sb = pp[:, HB:2*HB]
            lg_sb = pp[:, 2*HB:2*HB+8]
            lb_sb = pp[:, 2*HB+8:2*HB+16]
            b2_sb = pp[:, 2*HB+16:2*HB+24]
            wld[1].start(kt3(w1_sb, NKT), dr3(w1), serialize=False)
            ld.start(b1a_sb, fb1a_l[layer][:, :], serialize=False)
            ld.start(b1g_sb, fb1g_l[layer][:, :], serialize=False)
            ld.start(lg_sb, flg_l[layer][:, :], serialize=False)
            ld.start(lb_sb, flb_l[layer][:, :], serialize=False)
            ld.start(b2_sb, fb2_l[layer][:, :], serialize=False)
            for e in ph.eng.values():
                ld.wait_all(e); wld[1].wait_all(e)
            ph.barrier()

            for c in range(NTC):
                t0 = c * TC
                sy.wait_ge(ch.sem, ch.n); sy.wait_ge(mm.sem, mm.n)
                sy.wait_ge(chS.sem, chS.n); sy.wait_ge(chV.sem, chV.n)
                st.wait_all(sy)
                ld.start(kt3(xa_sb, NKT)[:, :, 1:TC+1],
                         dr3(src_xa)[:, :, t0:t0+TC])
                if c == 0:
                    ch.wait(v)
                    ch.inc(v.memset(kt3(xa_sb, NKT)[:, :, 0:1], 0.0))
                else:
                    ld.start(kt3(xa_sb, NKT)[:, :, 0:1],
                             dr3(src_xa)[:, :, t0-1:t0],
                             allow_slow_non_contiguous=True)
                if last:
                    ld.start(kt3(x1_sb, NKT), dr3(s_x1)[:, :, t0:t0+TC])
                for e in (t, v, s):
                    ld.wait_all(e); st.wait_all(e)
                def xsh(kt):
                    off = 1 if kt < 4 else 0
                    return xa_sb[:, kt*(TC+1)+off: kt*(TC+1)+off+TC]
                def xcur(kt):
                    return xa_sb[:, kt*(TC+1)+1: kt*(TC+1)+1+TC]
                ch.wait(t); ch.wait(s)
                ln_stats(ph, ch, [xsh(kt) for kt in range(NKT)], TC, P[6], P[7],
                         scrA, ones_col, mu1, rs1, scrow, eps_ap=eps_ap)
                bcast_rows(ph, ch, [(mu1, P[6]), (rs1, P[7])], TC, ones_row)
                ch.wait(v)
                act_done = {}
                for kt in range(NKT):
                    scr_o = scrB if kt % 2 == 0 else scrC
                    ch.wait(v)
                    if kt >= 2:
                        v.wait_ge(ch.sem, act_done[kt-2])
                    ch.inc(v.tensor_tensor(scrA[:, 0:TC], xsh(kt), P[6][:, 0:TC],
                                           ALU.subtract))
                    ch.wait(v)
                    ch.inc(v.tensor_tensor(scr_o[:, 0:TC], scrA[:, 0:TC],
                                           P[7][:, 0:TC], ALU.mult))
                    s.wait_ge(ch.sem, ch.n)
                    ch.inc(s.activation(out=tgt_bf[:, kt*TC:(kt+1)*TC],
                                        in_=scr_o[:, 0:TC], func=AF.Identity,
                                        scale=lg_sb[:, kt:kt+1],
                                        bias=lb_sb[:, kt:kt+1]))
                    act_done[kt] = ch.n
                tgt_ready = ch.n
                h_done = {}
                s_done = {}
                for hb in range(HB):
                    slot = hb % 2
                    bw = HBW[hb]
                    ps_a = P[0] if slot == 0 else P[2]
                    ps_g = P[1] if slot == 0 else P[3]
                    t.wait_ge(ch.sem, tgt_ready)
                    if hb >= 2:
                        t.wait_ge(chS.sem, s_done[hb-2])
                    for kt in range(NKT):
                        t.matmul(ps_a[0:bw, 0:TC],
                                 w1_sb[:, kt*5460 + hb*128: kt*5460 + hb*128 + bw],
                                 tgt_bf[:, kt*TC:(kt+1)*TC],
                                 start=(kt == 0), stop=(kt == NKT-1))
                    for kt in range(NKT):
                        i = t.matmul(ps_g[0:bw, 0:TC],
                                     w1_sb[:, kt*5460 + 2730 + hb*128:
                                           kt*5460 + 2730 + hb*128 + bw],
                                     tgt_bf[:, kt*TC:(kt+1)*TC],
                                     start=(kt == 0), stop=(kt == NKT-1))
                    mm.inc(i)
                    s.wait_ge(mm.sem, mm.n)
                    if hb >= 1:
                        s.wait_ge(chV.sem, h_done[hb-1])
                    s.activation(out=scrA[0:bw, 0:TC], in_=ps_a[0:bw, 0:TC],
                                 func=AF.Identity, bias=b1a_sb[0:bw, hb:hb+1])
                    chS.inc(s.activation(out=scrB[0:bw, 0:TC],
                                         in_=ps_g[0:bw, 0:TC],
                                         func=AF.Gelu, bias=b1g_sb[0:bw, hb:hb+1]))
                    s_done[hb] = chS.n
                    v.wait_ge(chS.sem, chS.n)
                    chV.inc(v.tensor_tensor(h_bf[0:bw, hb*TC:(hb+1)*TC],
                                            scrA[0:bw, 0:TC], scrB[0:bw, 0:TC],
                                            ALU.mult))
                    h_done[hb] = chV.n
                h_ready = chV.n
                add_done = {}
                w2_base = [wld[0].n, wld[1].n]
                mm_base_w2 = mm.n
                for ob in range(NKT):
                    slot = ob % 2
                    if ob >= 2:
                        g.wait_ge(mm.sem, mm_base_w2 + (ob - 1) * HB)
                    wslab = w2buf[:, slot*HB*128:(slot+1)*HB*128]
                    wld[slot].start(
                        wslab.rearrange("p (kt o) -> p kt o", kt=HB)[:, 0:21, :],
                        w2[0:2688, ob*128:(ob+1)*128].rearrange(
                            "(kt p) o -> p kt o", p=128))
                    wld[slot].start(
                        wslab.rearrange("p (kt o) -> p kt o", kt=HB)[0:42, 21:22, :],
                        w2[2688:2730, ob*128:(ob+1)*128].rearrange(
                            "(kt p) o -> p kt o", p=42))
                for ob in range(NKT):
                    slot = ob % 2
                    par = ob % 2
                    psw = P[4] if par == 0 else P[5]
                    t.wait_ge(chV.sem, h_ready)
                    t.wait_ge(wld[slot].sem, w2_base[slot] + 32*(ob//2 + 1))
                    if ob >= 2:
                        t.wait_ge(chS.sem, add_done[ob-2])
                    wslab = w2buf[:, slot*HB*128:(slot+1)*HB*128]
                    for hb in range(HB):
                        bw = HBW[hb]
                        mm.inc(t.matmul(psw[:, 0:TC],
                                        wslab[0:bw, hb*128:(hb+1)*128],
                                        h_bf[0:bw, hb*TC:(hb+1)*TC],
                                        start=(hb == 0), stop=(hb == HB-1)))
                    s.wait_ge(mm.sem, mm.n)
                    chS.inc(s.activation(out=scrA[:, 0:TC], in_=psw[:, 0:TC],
                                         func=AF.Identity, bias=b2_sb[:, ob:ob+1]))
                    v.wait_ge(chS.sem, chS.n)
                    chV.inc(v.tensor_tensor(out_sb[:, ob*TC:(ob+1)*TC],
                                            scrA[:, 0:TC], xcur(ob), ALU.add))
                    add_done[ob] = chS.n
                if last:
                    for e_ in (t, v, s):
                        e_.wait_ge(chV.sem, chV.n)
                        e_.wait_ge(chS.sem, chS.n)
                    def apply_last(kt):
                        ch.wait(v)
                        ch.inc(v.tensor_tensor(scrA[:, 0:TC],
                                               out_sb[:, kt*TC:(kt+1)*TC],
                                               P[6][:, 0:TC], ALU.subtract))
                        ch.wait(v)
                        ch.inc(v.tensor_tensor(scrB[:, 0:TC], scrA[:, 0:TC],
                                               P[7][:, 0:TC], ALU.mult))
                        ch.wait(v)
                        ch.inc(v.tensor_tensor(out_sb[:, kt*TC:(kt+1)*TC],
                                               scrB[:, 0:TC],
                                               x1_sb[:, kt*TC:(kt+1)*TC], ALU.add))
                    ln_block([out_sb[:, kt*TC:(kt+1)*TC] for kt in range(NKT)],
                             TC, apply_last)
                g.wait_ge(ch.sem, ch.n)
                g.wait_ge(chV.sem, chV.n)
                st.start(dr3(dst)[:, :, t0:t0+TC], kt3(out_sb, NKT))
                ph.barrier()
            stage_sync()

    def rwkv_stage(src_x2, dst_x3):
        with contextlib.ExitStack() as es:
            E = es.enter_context
            x2_sb = E(nc.sbuf_tensor("r_x2", [128, NKT * TC], F32))
            xrs_sb = E(nc.sbuf_tensor("r_xrs", [128, NKT * (TC + 1)], F32))
            xm_bf = E(nc.sbuf_tensor("r_xm", [128, NKT * TC], BF16))
            wk_sb = E(nc.sbuf_tensor("r_wk", [128, NKT * 1024], BF16))
            wv_sb = E(nc.sbuf_tensor("r_wv", [128, NKT * 1024], BF16))
            wr_sb = E(nc.sbuf_tensor("r_wr", [128, NKT * 1024], BF16))
            wobuf = E(nc.sbuf_tensor("r_wob", [128, 2 * NKT * 128], BF16))
            sr1_sb = E(nc.sbuf_tensor("r_sr1", [128, TC], F32))
            ks_sb = E(nc.sbuf_tensor("r_ks", [128, NKT * (TC + 1)], F32))
            kvs_sb = E(nc.sbuf_tensor("r_kvs", [128, NKT * (TC + 1)], F32))
            sk_sb = E(nc.sbuf_tensor("r_sk", [128, NKT * (TC + 1)], F32))
            skv_sb = E(nc.sbuf_tensor("r_skv", [128, NKT * (TC + 1)], F32))
            rw_bf = E(nc.sbuf_tensor("r_rw", [128, NKT * TC], BF16))
            ro_sb = E(nc.sbuf_tensor("r_ro", [128, NKT * TC], F32))
            pp = E(nc.sbuf_tensor("r_pp", [128, 40], F32))
            tm_sb = pp[:, 0:8]
            tm1_sb = pp[:, 8:16]
            dec_sb = pp[:, 16:24]
            tfx_sb = pp[:, 24:32]
            ld.start(tm_sb, tmd[:, :], serialize=False)
            ld.start(dec_sb, tdd[:, :], serialize=False)
            ld.start(tfx_sb, tfd[:, :], serialize=False)
            wld[0].start(kt3(wk_sb, NKT), dr3(wkTd), serialize=False)
            wld[0].start(kt3(wv_sb, NKT), dr3(wvTd), serialize=False)
            wld[0].start(kt3(wr_sb, NKT), dr3(wrTd), serialize=False)
            for e in ph.eng.values():
                ld.wait_all(e); wld[0].wait_all(e)
            ch.wait(v)
            ch.inc(v.tensor_scalar(tm1_sb, tm_sb, -1.0, 1.0, ALU.mult, ALU.add))
            ch.wait(s)
            ch.inc(s.activation(out=dec_sb, in_=dec_sb, func=AF.Exp))
            ch.wait(s)
            ch.inc(s.activation(out=dec_sb, in_=dec_sb, func=AF.Exp, scale=-1.0))
            ch.wait(s)
            ch.inc(s.activation(out=tfx_sb, in_=tfx_sb, func=AF.Exp))
            for buf in (ks_sb, kvs_sb, sk_sb, skv_sb, xrs_sb):
                ch.wait(v)
                ch.inc(v.memset(buf[:, :].rearrange(
                    "p (kt t) -> p kt t", kt=NKT)[:, :, 0:1], 0.0))
            ph.barrier()

            for c in range(NTC):
                t0 = c * TC
                sy.wait_ge(ch.sem, ch.n); sy.wait_ge(mm.sem, mm.n)
                st.wait_all(sy)
                ld.start(kt3(x2_sb, NKT), dr3(src_x2)[:, :, t0:t0+TC])
                for e in (t, v, s):
                    ld.wait_all(e); st.wait_all(e)
                x2kt = lambda kt: x2_sb[:, kt*TC:(kt+1)*TC]

                def applyr(kt):
                    ch.wait(v)
                    ch.inc(v.tensor_tensor(scrA[:, 0:TC], x2kt(kt), P[6][:, 0:TC],
                                           ALU.subtract))
                    ch.wait(v)
                    ch.inc(v.tensor_tensor(xrs_sb[:, kt*(TC+1)+1: kt*(TC+1)+1+TC],
                                           scrA[:, 0:TC], P[7][:, 0:TC], ALU.mult))
                ln_block([x2kt(kt) for kt in range(NKT)], TC, applyr)
                for kt in range(NKT):
                    ch.wait(v)
                    ch.inc(v.tensor_scalar(scrA[:, 0:TC],
                                           xrs_sb[:, kt*(TC+1)+1: kt*(TC+1)+1+TC],
                                           tm_sb[:, kt:kt+1], None, ALU.mult))
                    ch.wait(v)
                    ch.inc(v.scalar_tensor_tensor(xm_bf[:, kt*TC:(kt+1)*TC],
                                                  xrs_sb[:, kt*(TC+1): kt*(TC+1)+TC],
                                                  tm1_sb[:, kt:kt+1], scrA[:, 0:TC],
                                                  ALU.mult, ALU.add))
                ch.wait(v)
                ch.inc(v.tensor_copy(
                    kt3(xrs_sb, NKT)[:, :, 0:1], kt3(xrs_sb, NKT)[:, :, TC:TC+1]))
                xm_ready = ch.n
                for ob in range(NKT):
                    t.wait_ge(ch.sem, ch.n)
                    for kt in range(NKT):
                        t.matmul(P[0][:, 0:TC],
                                 wk_sb[:, kt*1024 + ob*128: kt*1024 + (ob+1)*128],
                                 xm_bf[:, kt*TC:(kt+1)*TC], start=(kt == 0),
                                 stop=(kt == NKT-1))
                    for kt in range(NKT):
                        t.matmul(P[1][:, 0:TC],
                                 wv_sb[:, kt*1024 + ob*128: kt*1024 + (ob+1)*128],
                                 xm_bf[:, kt*TC:(kt+1)*TC], start=(kt == 0),
                                 stop=(kt == NKT-1))
                    for kt in range(NKT):
                        i = t.matmul(P[2][:, 0:TC],
                                     wr_sb[:, kt*1024 + ob*128: kt*1024 + (ob+1)*128],
                                     xm_bf[:, kt*TC:(kt+1)*TC], start=(kt == 0),
                                     stop=(kt == NKT-1))
                    mm.inc(i)
                    v.wait_ge(mm.sem, mm.n)
                    kt = ob
                    ch.wait(v)
                    ch.inc(v.tensor_scalar(scrA[:, 0:TC], P[0][:, 0:TC],
                                           60.0, None, ALU.min))
                    ch.wait(s)
                    ch.inc(s.activation(out=ks_sb[:, kt*(TC+1)+1: kt*(TC+1)+1+TC],
                                        in_=scrA[:, 0:TC], func=AF.Exp))
                    ch.wait(s)
                    ch.inc(s.activation(out=scrB[:, 0:TC], in_=P[1][:, 0:TC],
                                        func=AF.Identity))
                    ch.wait(s)
                    ch.inc(s.activation(out=sr1_sb[:, 0:TC],
                                        in_=P[2][:, 0:TC], func=AF.Sigmoid))
                    ch.wait(v)
                    ch.inc(v.tensor_tensor(kvs_sb[:, kt*(TC+1)+1: kt*(TC+1)+1+TC],
                                           ks_sb[:, kt*(TC+1)+1: kt*(TC+1)+1+TC],
                                           scrB[:, 0:TC], ALU.mult))
                    d0 = dec_sb[:, kt:kt+1].broadcast_to([128, TC])
                    ch.wait(v)
                    ch.inc(v.tensor_tensor_scan(
                        sk_sb[:, kt*(TC+1)+1: kt*(TC+1)+1+TC], d0,
                        ks_sb[:, kt*(TC+1): kt*(TC+1)+TC],
                        sk_sb[:, kt*(TC+1): kt*(TC+1)+1], ALU.mult, ALU.add))
                    ch.wait(v)
                    ch.inc(v.tensor_tensor_scan(
                        skv_sb[:, kt*(TC+1)+1: kt*(TC+1)+1+TC], d0,
                        kvs_sb[:, kt*(TC+1): kt*(TC+1)+TC],
                        skv_sb[:, kt*(TC+1): kt*(TC+1)+1], ALU.mult, ALU.add))
                    ch.wait(v)
                    ch.inc(v.tensor_copy(sk_sb[:, kt*(TC+1): kt*(TC+1)+1],
                                         sk_sb[:, kt*(TC+1)+TC: kt*(TC+1)+TC+1]))
                    ch.wait(v)
                    ch.inc(v.tensor_copy(skv_sb[:, kt*(TC+1): kt*(TC+1)+1],
                                         skv_sb[:, kt*(TC+1)+TC: kt*(TC+1)+TC+1]))
                    ch.wait(v)
                    ch.inc(v.tensor_copy(ks_sb[:, kt*(TC+1): kt*(TC+1)+1],
                                         ks_sb[:, kt*(TC+1)+TC: kt*(TC+1)+TC+1]))
                    ch.wait(v)
                    ch.inc(v.tensor_copy(kvs_sb[:, kt*(TC+1): kt*(TC+1)+1],
                                         kvs_sb[:, kt*(TC+1)+TC: kt*(TC+1)+TC+1]))
                    ch.wait(v)
                    ch.inc(v.scalar_tensor_tensor(
                        skv_sb[:, kt*(TC+1)+1: kt*(TC+1)+1+TC],
                        kvs_sb[:, kt*(TC+1)+1: kt*(TC+1)+1+TC],
                        tfx_sb[:, kt:kt+1],
                        skv_sb[:, kt*(TC+1)+1: kt*(TC+1)+1+TC],
                        ALU.mult, ALU.add))
                    ch.wait(v)
                    ch.inc(v.scalar_tensor_tensor(
                        scrA[:, 0:TC],
                        ks_sb[:, kt*(TC+1)+1: kt*(TC+1)+1+TC],
                        tfx_sb[:, kt:kt+1],
                        sk_sb[:, kt*(TC+1)+1: kt*(TC+1)+1+TC],
                        ALU.mult, ALU.add))
                    ch.wait(v)
                    ch.inc(v.tensor_scalar(scrA[:, 0:TC], scrA[:, 0:TC], 1e-16,
                                           None, ALU.add))
                    ch.wait(v)
                    ch.inc(v.reciprocal(scrA[:, 0:TC], scrA[:, 0:TC]))
                    ch.wait(v)
                    ch.inc(v.tensor_tensor(scrB[:, 0:TC],
                                           skv_sb[:, kt*(TC+1)+1: kt*(TC+1)+1+TC],
                                           scrA[:, 0:TC], ALU.mult))
                    ch.wait(v)
                    ch.inc(v.tensor_tensor(rw_bf[:, kt*TC:(kt+1)*TC],
                                           scrB[:, 0:TC],
                                           sr1_sb[:, 0:TC], ALU.mult))
                rw_ready = ch.n
                mm_base_wo = mm.n
                for ob in range(NKT):
                    slot = ob % 2
                    if ob >= 2:
                        g.wait_ge(mm.sem, mm_base_wo + ob - 1)
                    wld[slot].start(
                        wobuf[:, slot*NKT*128:(slot+1)*NKT*128].rearrange(
                            "p (kt o) -> p kt o", kt=NKT),
                        dr3(woTd)[:, :, ob*128:(ob+1)*128])
                wo_base = [wld[0].n - (NKT//2)*16, wld[1].n - (NKT//2)*16]
                evd = {}
                for ob in range(NKT):
                    slot = ob % 2
                    t.wait_ge(ch.sem, rw_ready)
                    if ob >= 1:
                        t.wait_ge(ch.sem, evd[ob-1])
                    t.wait_ge(wld[slot].sem, wo_base[slot] + 16*(ob//2 + 1))
                    for kt in range(NKT):
                        i = t.matmul(P[3][:, 0:TC],
                                     wobuf[:, slot*NKT*128 + kt*128:
                                           slot*NKT*128 + (kt+1)*128],
                                     rw_bf[:, kt*TC:(kt+1)*TC], start=(kt == 0),
                                     stop=(kt == NKT-1))
                    mm.inc(i)
                    s.wait_ge(mm.sem, mm.n)
                    ch.wait(s)
                    ch.inc(s.activation(out=ro_sb[:, ob*TC:(ob+1)*TC],
                                        in_=P[3][:, 0:TC], func=AF.Identity))
                    evd[ob] = ch.n

                def applyro(kt):
                    ch.wait(v)
                    ch.inc(v.tensor_tensor(scrA[:, 0:TC],
                                           ro_sb[:, kt*TC:(kt+1)*TC],
                                           P[6][:, 0:TC], ALU.subtract))
                    ch.wait(v)
                    ch.inc(v.tensor_tensor(scrB[:, 0:TC], scrA[:, 0:TC],
                                           P[7][:, 0:TC], ALU.mult))
                    ch.wait(v)
                    ch.inc(v.tensor_tensor(ro_sb[:, kt*TC:(kt+1)*TC],
                                           scrB[:, 0:TC], x2kt(kt), ALU.add))
                ln_block([ro_sb[:, kt*TC:(kt+1)*TC] for kt in range(NKT)], TC,
                         applyro)
                g.wait_ge(ch.sem, ch.n)
                st.start(dr3(dst_x3)[:, :, t0:t0+TC], kt3(ro_sb, NKT))
                ph.barrier()
            stage_sync()

    def l1ffn_stage(src_x3, dst):
        HB32 = 32
        with contextlib.ExitStack() as es:
            E = es.enter_context
            x3_sb = E(nc.sbuf_tensor("l_x3", [128, NKT * TC], F32))
            tgt_bf = E(nc.sbuf_tensor("l_tg", [128, NKT * TC], BF16))
            h_bf = E(nc.sbuf_tensor("l_h", [128, HB32 * TC], BF16))
            eo_sb = E(nc.sbuf_tensor("l_eo", [128, NKT * TC], F32))
            xi_sb = E(nc.sbuf_tensor("l_xi", [128, NKT * TC], F32))
            w1_sb = E(nc.sbuf_tensor("l_w1", [128, NKT * 4096], BF16))
            w2buf = E(nc.sbuf_tensor("l_w2", [128, 2 * 512], BF16))
            pp = E(nc.sbuf_tensor("l_pp", [128, 60], F32))
            b1_sb = pp[:, 0:32]
            b2_sb = pp[:, 32:40]
            gm_sb = pp[:, 40:48]
            bt_sb = pp[:, 48:56]
            wld[1].start(kt3(w1_sb, NKT), dr3(lw1), serialize=False)
            ld.start(b1_sb, lb1d[:, :], serialize=False)
            ld.start(b2_sb, lb2d[:, :], serialize=False)
            ld.start(gm_sb, lgmd[:, :], serialize=False)
            ld.start(bt_sb, lbtd[:, :], serialize=False)
            for e in ph.eng.values():
                ld.wait_all(e)
            ph.barrier()

            for c in range(NTC):
                t0 = c * TC
                sy.wait_ge(ch.sem, ch.n); sy.wait_ge(mm.sem, mm.n)
                sy.wait_ge(chS.sem, chS.n); sy.wait_ge(chV.sem, chV.n)
                st.wait_all(sy); st2.wait_all(sy)
                ld.start(kt3(x3_sb, NKT), dr3(src_x3)[:, :, t0:t0+TC])
                for e in (t, v, s):
                    ld.wait_all(e); st.wait_all(e); st2.wait_all(e)
                x3kt = lambda kt: x3_sb[:, kt*TC:(kt+1)*TC]

                def applyl1(kt):
                    ch.wait(v)
                    ch.inc(v.tensor_tensor(scrA[:, 0:TC], x3kt(kt), P[6][:, 0:TC],
                                           ALU.subtract))
                    ch.wait(v)
                    ch.inc(v.tensor_tensor(tgt_bf[:, kt*TC:(kt+1)*TC],
                                           scrA[:, 0:TC], P[7][:, 0:TC], ALU.mult))
                ln_block([x3kt(kt) for kt in range(NKT)], TC, applyl1)
                tgt_ready = ch.n
                ch.wait(v)
                ch.inc(v.tensor_copy(rows[0:1, 1536:2048], rows[0:1, 0:512]))
                mu_save = rows[0:1, 1536:2048]
                ch.wait(v)
                ch.inc(v.tensor_copy(rows2[0:1, 0:512], rs1))
                rs_save = rows2[0:1, 0:512]
                h_done = {}
                for hb in range(HB32):
                    slot = hb % 2
                    ps_h = P[0] if slot == 0 else P[1]
                    t.wait_ge(ch.sem, tgt_ready)
                    if hb >= 2:
                        t.wait_ge(chS.sem, h_done[hb-2])
                    for kt in range(NKT):
                        i = t.matmul(ps_h[:, 0:TC],
                                     w1_sb[:, kt*4096 + hb*128:
                                           kt*4096 + (hb+1)*128],
                                     tgt_bf[:, kt*TC:(kt+1)*TC],
                                     start=(kt == 0), stop=(kt == NKT-1))
                    mm.inc(i)
                    s.wait_ge(mm.sem, mm.n)
                    chS.inc(s.activation(out=h_bf[:, hb*TC:(hb+1)*TC],
                                         in_=ps_h[:, 0:TC], func=AF.Relu,
                                         bias=b1_sb[:, hb:hb+1]))
                    h_done[hb] = chS.n
                w2_base = [wld[0].n, wld[1].n]
                mm_base_w2 = mm.n
                for obg in range(2):
                    for kt2 in range(HB32):
                        idx = obg*HB32 + kt2
                        slot = idx % 2
                        if idx >= 2:
                            g.wait_ge(mm.sem, mm_base_w2 + idx - 1)
                        wld[slot].start(w2buf[:, slot*512:(slot+1)*512],
                                        lw2[kt2*128:(kt2+1)*128,
                                            obg*512:(obg+1)*512])
                eo_done = {}
                for obg in range(2):
                    if obg == 1:
                        t.wait_ge(ch.sem, eo_done[0])
                    t.wait_ge(ch.sem, ch.n)
                    t.wait_ge(chS.sem, chS.n)
                    for kt2 in range(HB32):
                        idx = obg*HB32 + kt2
                        slot = idx % 2
                        t.wait_ge(wld[slot].sem, w2_base[slot] + 16*(idx//2 + 1))
                        for ob4 in range(4):
                            i = t.matmul(P[2 + ob4][:, 0:TC],
                                         w2buf[:, slot*512 + ob4*128:
                                               slot*512 + (ob4+1)*128],
                                         h_bf[:, kt2*TC:(kt2+1)*TC],
                                         start=(kt2 == 0), stop=(kt2 == HB32-1))
                        mm.inc(i)
                    s.wait_ge(mm.sem, mm.n)
                    for ob4 in range(4):
                        ko = obg*4 + ob4
                        ch.wait(s)
                        ch.inc(s.activation(out=eo_sb[:, ko*TC:(ko+1)*TC],
                                            in_=P[2 + ob4][:, 0:TC],
                                            func=AF.Identity,
                                            bias=b2_sb[:, ko:ko+1]))
                    eo_done[obg] = ch.n
                bcast_rows(ph, ch, [(mu_save, P[6]), (rs_save, P[7])], TC,
                           ones_row)
                ch.wait(v)
                for kt in range(NKT):
                    ch.wait(v)
                    ch.inc(v.tensor_tensor(scrA[:, 0:TC], x3kt(kt), P[6][:, 0:TC],
                                           ALU.subtract))
                    ch.wait(v)
                    ch.inc(v.tensor_tensor(scrB[:, 0:TC], scrA[:, 0:TC],
                                           P[7][:, 0:TC], ALU.mult))
                    ch.wait(v)
                    ch.inc(v.tensor_tensor(xi_sb[:, kt*TC:(kt+1)*TC],
                                           scrB[:, 0:TC],
                                           eo_sb[:, kt*TC:(kt+1)*TC], ALU.add))
                ch.wait(t); ch.wait(s)
                ln_stats(ph, ch, [xi_sb[:, kt*TC:(kt+1)*TC] for kt in range(NKT)],
                         TC, P[6], P[7], scrA, ones_col, mu1, rs1, scrow,
                         eps_ap=eps_ap)
                bcast_rows(ph, ch, [(mu1, P[6]), (rs1, P[7])], TC, ones_row)
                ch.wait(v)
                act_done = {}
                for kt in range(NKT):
                    scr_o = scrB if kt % 2 == 0 else scrC
                    ch.wait(v)
                    if kt >= 2:
                        v.wait_ge(ch.sem, act_done[kt-2])
                    ch.inc(v.tensor_tensor(scrA[:, 0:TC],
                                           xi_sb[:, kt*TC:(kt+1)*TC],
                                           P[6][:, 0:TC], ALU.subtract))
                    ch.wait(v)
                    ch.inc(v.tensor_tensor(scr_o[:, 0:TC], scrA[:, 0:TC],
                                           P[7][:, 0:TC], ALU.mult))
                    s.wait_ge(ch.sem, ch.n)
                    ch.inc(s.activation(out=eo_sb[:, kt*TC:(kt+1)*TC],
                                        in_=scr_o[:, 0:TC], func=AF.Identity,
                                        scale=gm_sb[:, kt:kt+1],
                                        bias=bt_sb[:, kt:kt+1]))
                    act_done[kt] = ch.n
                def applyl3(kt):
                    ch.wait(v)
                    ch.inc(v.tensor_tensor(scrA[:, 0:TC],
                                           eo_sb[:, kt*TC:(kt+1)*TC],
                                           P[6][:, 0:TC], ALU.subtract))
                    ch.wait(v)
                    ch.inc(v.tensor_tensor(scrB[:, 0:TC], scrA[:, 0:TC],
                                           P[7][:, 0:TC], ALU.mult))
                    ch.wait(v)
                    ch.inc(v.tensor_tensor(xi_sb[:, kt*TC:(kt+1)*TC],
                                           scrB[:, 0:TC], x3kt(kt), ALU.add))
                ch.wait(t); ch.wait(s)
                ln_stats(ph, ch, [eo_sb[:, kt*TC:(kt+1)*TC] for kt in range(NKT)],
                         TC, P[6], P[7], scrA, ones_col, mu1, rs1, scrow,
                         eps_ap=eps_ap)
                bcast_rows(ph, ch, [(mu1, P[6]), (rs1, P[7])], TC, ones_row)
                ch.wait(v)
                for kt in range(NKT):
                    applyl3(kt)
                g.wait_ge(ch.sem, ch.n)
                st.start(dr3(dst)[:, :, t0:t0+TC], kt3(xi_sb, NKT))
                ph.barrier()
            stage_sync()

    attn_stage(0, s_xp, s_xa)
    geglu_stage(0, s_xa, s_xp, last=False)
    attn_stage(1, s_xp, s_xa)
    geglu_stage(1, s_xa, s_x2, last=True)
    rwkv_stage(s_x2, s_x3)
    l1ffn_stage(s_x3, o_x4)
    for e in ph.eng.values():
        st.wait_all(e); st2.wait_all(e)
    ph.close()
    esg.close()
    return nc




import ml_dtypes
from concourse.bass_utils import run_bass_kernel_spmd

_T = 8192
_CORES = list(range(8))


def kernel(**ins):
    x0 = np.asarray(ins["L0_attn_x"], np.float32)
    nc = bass.Bass("TRN2", target_bir_lowering=False)
    build_fused(nc, T=_T, TC=512)

    ones = np.ones((128, 128), np.float32)
    consts = np.array([[1e-5, 0, 0, 0, 0, 0, 0, 0]], np.float32)
    cw = np.zeros((128, 64), np.float32)
    conv_w = ins["L0_attn_conv_w"][:, 0, :]
    for ob in range(16):
        for k in range(4):
            cw[:, 4 * ob + k] = conv_w[ob * 128:(ob + 1) * 128, k]
    E = np.zeros((128, 2048), np.float32)
    R = np.zeros((128, 2048), np.float32)
    for st16 in range(16):
        for p in range(128):
            j = p // 16
            E[8 * st16 + j, st16 * 128 + p] = 1.0
            R[p, st16 * 128 + 8 * st16 + j] = 1.0
    RepBC = np.zeros((16, 128), np.float32)
    for p in range(128):
        RepBC[p % 16, p] = 1.0
    dsk = ins["L0_attn_D_skip"]
    dskd = np.stack([dsk[i * 128:(i + 1) * 128] for i in range(16)], 1
                    ).astype(np.float32)

    half = np.arange(0, 64, 2, dtype=np.float32)
    inv_freq = 1.0 / 10000.0 ** (half / 64)
    base = (half + 0.4 * 64) / (1.4 * 64)
    pos = np.arange(_T, dtype=np.float32)
    fr = pos[:, None] * inv_freq[None, :]
    fr = np.concatenate([fr, fr], -1)
    power = (pos - _T // 2) / 256.0
    sc = base[None, :] ** power[:, None]
    sc = np.concatenate([sc, sc], -1)
    cq, sq_ = np.cos(fr), np.sin(fr)
    cosq8 = (8 * cq * sc).T
    sinq8 = (8 * sq_ * sc).T
    cosk = (cq / sc).T
    sink = (sq_ / sc).T
    tabq = np.concatenate([np.tile(cosq8, (2, 1)).reshape(128, _T),
                           np.tile(sinq8, (2, 1)).reshape(128, _T)],
                          1).astype(ml_dtypes.bfloat16)
    tabk = np.concatenate([np.tile(cosk, (2, 1)).reshape(128, _T),
                           np.tile(sink, (2, 1)).reshape(128, _T)],
                          1).astype(ml_dtypes.bfloat16)
    mg = np.zeros((128, 8 * 512), np.float32)
    jj = np.arange(512)
    for bi in range(4):
        for chunk in range(2):
            blk = np.zeros((128, 512), np.float32)
            for p in range(128):
                i = 128 * bi + p
                j1024 = chunk * 512 + jj
                ok = (j1024 >= i) & (j1024 <= i + 512)
                blk[p] = np.where(ok, 0.0, -1e30)
            mg[:, (bi * 2 + chunk) * 512:(bi * 2 + chunk + 1) * 512] = blk
    mg = mg.astype(ml_dtypes.bfloat16)
    rot = np.zeros((128, 128), np.float32)
    for blk2 in range(2):
        for m_ in range(64):
            if m_ < 32:
                rot[blk2 * 64 + m_ + 32, blk2 * 64 + m_] = -1.0
            else:
                rot[blk2 * 64 + m_ - 32, blk2 * 64 + m_] = 1.0
    bo2 = np.zeros((128, 2), np.float32)
    bc2 = np.zeros((2, 128), np.float32)
    for p in range(128):
        bo2[p, p // 64] = 1.0
        bc2[p // 64, p] = 1.0
    HBW = [128] * 21 + [42]

    def packr(vec):
        out = np.zeros((128, 22), np.float32)
        off = 0
        for i, w in enumerate(HBW):
            out[:w, i] = vec[off:off + w]
            off += w
        return out
    sel_np = np.zeros((128, 2048), np.float32)
    for st in range(16):
        for p in range(128):
            sel_np[st * 8 + p // 16, st * 128 + p] = 1.0

    tm = np.asarray(ins["L1_attn_time_mix"][0, 0], np.float32)
    td_rep = np.repeat(ins["L1_attn_time_decay"][:, 0], 64)
    tf_rep = np.repeat(ins["L1_attn_time_first"][:, 0], 64)

    common = {
        "w_in": ins["L0_attn_W_in"], "w_xp": ins["L0_attn_W_xproj"],
        "w_dt": ins["L0_attn_W_dt"], "cwd": cw,
        "cbd": pack_vec(ins["L0_attn_conv_b"], 16),
        "bdtd": pack_vec(ins["L0_attn_b_dt"], 16),
        "alogd": np.ascontiguousarray(ins["L0_attn_A_log"].reshape(-1)),
        "ed": E, "rd": R.astype(ml_dtypes.bfloat16), "rbcd": RepBC, "dskd": dskd,
        "woutd": ins["L0_attn_W_out"],
        "wqkv0": ins["L0_ffn_wqkv"][0], "wqkv1": ins["L0_ffn_wqkv"][1],
        "wo0": ins["L0_ffn_wo"][0], "wo1": ins["L0_ffn_wo"][1],
        "qsd0": np.stack([np.tile(ins["L0_ffn_q_scale"][0], 2),
                          np.tile(ins["L0_ffn_k_scale"][0], 2)], 1
                         ).astype(np.float32),
        "qsd1": np.stack([np.tile(ins["L0_ffn_q_scale"][1], 2),
                          np.tile(ins["L0_ffn_k_scale"][1], 2)], 1
                         ).astype(np.float32),
        "tabqd": tabq, "tabkd": tabk, "mgend": mg,
        "identd": np.eye(128).astype(ml_dtypes.bfloat16),
        "rotd": rot, "bo2d": bo2, "bc2d": bc2,
        "fw1_0": ins["L0_ffn_w1"][0], "fw1_1": ins["L0_ffn_w1"][1],
        "fw2_0": ins["L0_ffn_w2"][0], "fw2_1": ins["L0_ffn_w2"][1],
        "fb1a0": packr(ins["L0_ffn_b1"][0][:2730]),
        "fb1g0": packr(ins["L0_ffn_b1"][0][2730:]),
        "fb1a1": packr(ins["L0_ffn_b1"][1][:2730]),
        "fb1g1": packr(ins["L0_ffn_b1"][1][2730:]),
        "flg0": pack_vec(ins["L0_ffn_ln_g"][0], 8),
        "flb0": pack_vec(ins["L0_ffn_ln_b"][0], 8),
        "flg1": pack_vec(ins["L0_ffn_ln_g"][1], 8),
        "flb1": pack_vec(ins["L0_ffn_ln_b"][1], 8),
        "fb2_0": pack_vec(ins["L0_ffn_b2"][0], 8),
        "fb2_1": pack_vec(ins["L0_ffn_b2"][1], 8),
        "wkTd": np.ascontiguousarray(ins["L1_attn_Wk"].T),
        "wvTd": np.ascontiguousarray(ins["L1_attn_Wv"].T),
        "wrTd": np.ascontiguousarray(ins["L1_attn_Wr"].T),
        "woTd": np.ascontiguousarray(ins["L1_attn_Wo"].T),
        "tmd": pack_vec(tm, 8), "tdd": pack_vec(td_rep, 8),
        "tfd": pack_vec(tf_rep, 8),
        "lw1": ins["L1_ffn_w1"][0], "lw2": ins["L1_ffn_w2"][0],
        "lb1d": pack_vec(ins["L1_ffn_b1"][0], 32),
        "lb2d": pack_vec(ins["L1_ffn_b2"][0], 8),
        "lgmd": pack_vec(ins["L1_ffn_gamma"], 8),
        "lbtd": pack_vec(ins["L1_ffn_beta"], 8),
        "onesd": ones, "constsd": consts, "seld": sel_np,
    }
    maps = []
    for c in _CORES:
        b = c // 4
        mp = dict(common)
        mp["xin"] = np.ascontiguousarray(x0[b].T)
        maps.append(mp)
    res = run_bass_kernel_spmd(nc, maps, core_ids=_CORES)
    out = np.zeros((2, _T, 1024), np.float32)
    for c in _CORES:
        b, q = c // 4, c % 4
        out[b, q * 2048:(q + 1) * 2048] = res.results[c]["o_x4"][:, q * 2048:
                                                                (q + 1) * 2048].T
    return out
